# version 01
# baseline (speedup 1.0000x reference)
import contextlib
import numpy as np
import ml_dtypes
import concourse.bass as bass
import concourse.mybir as mybir
from concourse.bass import IndirectOffsetOnAxis, ds
from concourse.bass_utils import run_bass_kernel_spmd
from concourse.expressions import RuntimeValue

BF = ml_dtypes.bfloat16
F32 = mybir.dt.float32
BF16 = mybir.dt.bfloat16
U32 = mybir.dt.uint32
I32 = mybir.dt.int32
AF = mybir.ActivationFunctionType
OP = mybir.AluOpType
AX = mybir.AxisListType

NC = 8
B = 2
S = 8192
D = 1024
T = B * S
TPC = T // NC
NT = TPC // 128
DC = D // 128
E0, K0 = 8, 2
C0 = 4301
C0P = 4352
E1, K1 = 16, 2
C1 = 2560
H1, HD = 16, 64
WIN = 256


class Sem:
    def __init__(self, nc, es, name):
        self.h = es.enter_context(nc.semaphore(name))
        self.n = 0

    def inc(self, inst, k=1):
        inst.then_inc(self.h, k)
        self.n += k
        return self.n


def wait(eng, sem, n=None):
    eng.wait_ge(sem.h, sem.n if n is None else n)



def build_launchA():
    nc = bass.Bass("TRN2", target_bir_lowering=False, debug=False)
    x_sh = nc.dram_tensor("x_sh", [TPC, D], F32, kind="ExternalInput").ap()
    wg_in = nc.dram_tensor("wg", [E0, D], F32, kind="ExternalInput").ap()
    o_xnbf = nc.dram_tensor("o_xnbf", [TPC, D], BF16, kind="ExternalOutput").ap()
    o_uT = nc.dram_tensor("o_uT", [D, TPC], BF16, kind="ExternalOutput").ap()
    o_lg = nc.dram_tensor("o_lg", [TPC, E0], F32, kind="ExternalOutput").ap()

    es = contextlib.ExitStack()
    with es:
        sb = lambda name, shape, dt: es.enter_context(nc.sbuf_tensor(name, shape, dt))
        psa = lambda name, shape, dt: es.enter_context(nc.psum_tensor(name, shape, dt))
        io128 = sb("io128", [128, 128], F32)
        cid1 = sb("cid1", [128, 1], F32)
        identf = sb("identf", [128, 128], F32)
        identb = sb("identb", [128, 128], BF16)
        wgf = sb("wgf", [E0, D], F32)
        wghi = sb("wghi", [E0, D], BF16)
        wglo = sb("wglo", [E0, D], BF16)
        wgThi = sb("wgThi", [128, DC, E0], BF16)
        wgTlo = sb("wgTlo", [128, DC, E0], BF16)
        xt = [sb(f"xt{j}", [128, D], F32) for j in range(2)]
        xc = sb("xc", [128, D], F32)
        xnf = [sb(f"xnf{j}", [128, D], F32) for j in range(2)]
        sq = sb("sq", [128, D], F32)
        xnbf = [sb(f"xnbf{j}", [128, D], BF16) for j in range(2)]
        ubf = [sb(f"ubf{j}", [128, D], BF16) for j in range(2)]
        st_ = sb("stats", [128, 8], F32)
        st2_ = sb("stats2", [128, 4], F32)
        epsc = sb("epsc", [128, 1], F32)
        hiT = sb("hiT", [128, DC * 128], BF16)
        loT = sb("loT", [128, DC * 128], BF16)
        uTsb = sb("uTsb", [128, DC * TPC], BF16)
        lgsb = sb("lgsb", [128, E0], F32)
        psT = [psa(f"psT{j}", [128, 128], F32) for j in range(4)]
        psB = [psa(f"psB{j}", [128, 128], BF16) for j in range(2)]
        psL = [psa(f"psL{j}", [128, E0], F32) for j in range(2)]

        s_x = [Sem(nc, es, "s_x0"), Sem(nc, es, "s_x1")]
        s_w = Sem(nc, es, "s_w")
        s_v = Sem(nc, es, "s_v")
        s_a = Sem(nc, es, "s_a")
        s_t = Sem(nc, es, "s_t")
        s_g = Sem(nc, es, "s_g")
        s_gx = Sem(nc, es, "s_gx")
        s_gl = Sem(nc, es, "s_gl")
        s_gu = Sem(nc, es, "s_gu")

        block = es.enter_context(nc.Block())

        @block.sync
        def _(sy):
            s_w.inc(sy.dma_start(out=wgf[:, :], in_=wg_in[:, :]), 16)

        @block.gpsimd
        def _(g):
            g.iota(io128[:, :], pattern=[[1, 128]], base=0, channel_multiplier=0,
                   allow_small_or_imprecise_dtypes=True)
            s_g.inc(g.iota(cid1[:, :], pattern=[[0, 1]], base=0, channel_multiplier=1,
                           allow_small_or_imprecise_dtypes=True))

        @block.vector
        def _(v):
            s_v.inc(v.memset(epsc[:, :], 1e-5))
            wait(v, s_g)
            s_v.inc(v.tensor_scalar(identf[:, :], io128[:, :], cid1[:, :], None, OP.is_equal))
            wait(v, s_v)
            s_v.inc(v.tensor_copy(identb[:, :], identf[:, :]))
            wait(v, s_w, 16)
            s_v.inc(v.tensor_copy(wghi[:, :], wgf[:, :]))
            wait(v, s_v)
            s_v.inc(v.tensor_tensor(wglo[:, :], wgf[:, :], wghi[:, :], OP.subtract))
        v_wg = s_v.n

        for dc in range(DC):
            @block.tensor
            def _(t, dc=dc):
                if dc == 0:
                    wait(t, s_v, v_wg)
                if dc >= 1:
                    wait(t, s_v, wg_ev[dc - 1])
                s_t.inc(t.transpose(psB[0][:, 0:E0], wghi[0:E0, dc * 128:(dc + 1) * 128], identb[0:E0, 0:E0]))
                s_t.inc(t.transpose(psB[1][:, 0:E0], wglo[0:E0, dc * 128:(dc + 1) * 128], identb[0:E0, 0:E0]))

            @block.vector
            def _(v, dc=dc):
                wait(v, s_t)
                s_v.inc(v.tensor_copy(wgThi[:, dc, :], psB[0][:, 0:E0]))
                s_v.inc(v.tensor_copy(wgTlo[:, dc, :], psB[1][:, 0:E0]))
                wg_ev.append(s_v.n)

        v_xc = [0] * NT
        v_xn = [0] * NT
        v_fin = [0] * NT
        v_lg = [0] * NT
        t_utr = [0] * NT
        t_lg = [0] * NT
        go_xn = [0] * NT
        go_lg = [0] * NT

        for i in range(NT):
            bi = i % 2

            @block.sync
            def _(sy, i=i, bi=bi):
                if i >= 2:
                    wait(sy, s_v, v_xc[i - 2])
                s_x[bi].inc(sy.dma_start(out=xt[bi][:, :], in_=x_sh[i * 128:(i + 1) * 128, :]), 16)

            @block.vector
            def _(v, i=i, bi=bi):
                wait(v, s_x[bi], 16 * (i // 2 + 1))
                s_v.inc(v.tensor_reduce(st_[:, 0:1], xt[bi][:, :], AX.X, OP.add))
                wait(v, s_v)
                s_v.inc(v.tensor_scalar(st_[:, 1:2], st_[:, 0:1], 1.0 / D, None, OP.mult))
                wait(v, s_v)
                if i >= 1:
                    wait(v, s_a, a_sq[i - 1])
                s_v.inc(v.tensor_scalar(xc[:, :], xt[bi][:, :], st_[:, 1:2], None, OP.subtract))
                v_xc[i] = s_v.n

            @block.scalar
            def _(a, i=i):
                wait(a, s_v, v_xc[i])
                s_a.inc(a.activation(out=sq[:, :], in_=xc[:, :], func=AF.Square, accum_out=st_[:, 2:3]))
                a_sq[i] = s_a.n

            @block.vector
            def _(v, i=i):
                wait(v, s_a, a_sq[i])
                s_v.inc(v.tensor_scalar(st_[:, 3:4], st_[:, 2:3], 1.0 / D, None, OP.mult))

            @block.scalar
            def _(a, i=i):
                wait(a, s_v)
                s_a.inc(a.activation(out=st_[:, 4:5], in_=st_[:, 3:4], func=AF.Sqrt, bias=epsc[:, 0:1]))
                a_std[i] = s_a.n

            @block.vector
            def _(v, i=i, bi=bi):
                wait(v, s_a, a_std[i])
                s_v.inc(v.reciprocal(st_[:, 5:6], st_[:, 4:5]))
                wait(v, s_v)
                if i >= 2:
                    wait(v, s_t, t_lg[i - 2])
                s_v.inc(v.tensor_scalar(xnf[bi][:, :], xc[:, :], st_[:, 5:6], None, OP.mult))
                v_xn[i] = s_v.n

            @block.scalar
            def _(a, i=i, bi=bi):
                wait(a, s_v, v_xn[i])
                s_a.inc(a.activation(out=sq[:, :], in_=xnf[bi][:, :], func=AF.Square, accum_out=st_[:, 6:7]))
                wait(a, s_a)
                s_a.inc(a.activation(out=st_[:, 7:8], in_=st_[:, 6:7], func=AF.Sqrt))
                a_n[i] = s_a.n

            @block.vector
            def _(v, i=i, bi=bi):
                wait(v, s_a, a_n[i])
                s_v.inc(v.tensor_scalar(st2_[:, 0:1], st_[:, 7:8], 1e-20, None, OP.max))
                wait(v, s_v)
                s_v.inc(v.reciprocal(st2_[:, 1:2], st2_[:, 0:1]))
                wait(v, s_v)
                if i >= 2:
                    wait(v, s_gx, 16 * i)
                s_v.inc(v.tensor_copy(xnbf[bi][:, :], xnf[bi][:, :]))
                wait(v, s_v)
                if i >= 2:
                    wait(v, s_t, t_utr[i - 2])
                s_v.inc(v.tensor_scalar(ubf[bi][:, :], xnf[bi][:, :], st2_[:, 1:2], None, OP.mult))
                v_fin[i] = s_v.n

            @block.gpsimd
            def _(g, i=i, bi=bi):
                wait(g, s_v, v_fin[i])
                go_xn[i] = s_gx.inc(g.dma_start(out=o_xnbf[i * 128:(i + 1) * 128, :], in_=xnbf[bi][:, :]), 16)

            for dc in range(DC):
                @block.tensor
                def _(t, i=i, bi=bi, dc=dc):
                    if dc == 0:
                        wait(t, s_v, v_xn[i])
                    if dc >= 4:
                        wait(t, s_v, hilo_ev[dc - 4])
                    elif i >= 1 and dc < 4:
                        wait(t, s_v, v_lg[i - 1])
                    s_t.inc(t.transpose(psT[dc % 4][:, :], xnf[bi][:, dc * 128:(dc + 1) * 128], identf[:, :]))

                @block.vector
                def _(v, i=i, dc=dc):
                    wait(v, s_t)
                    s_v.inc(v.tensor_copy(hiT[:, dc * 128:(dc + 1) * 128], psT[dc % 4][:, :]))
                    wait(v, s_v)
                    s_v.inc(v.tensor_tensor(loT[:, dc * 128:(dc + 1) * 128], psT[dc % 4][:, :],
                                            hiT[:, dc * 128:(dc + 1) * 128], OP.subtract))
                    hilo_ev[dc] = s_v.n
            v_hilo = s_v.n

            for dc in range(DC):
                @block.tensor
                def _(t, i=i, bi=bi, dc=dc):
                    if dc == 0:
                        wait(t, s_v, v_fin[i])
                    if dc >= 2:
                        wait(t, s_v, u_ev[dc - 2])
                    elif i >= 1:
                        wait(t, s_v, v_lg[i - 1])
                    s_t.inc(t.transpose(psB[dc % 2][:, :], ubf[bi][:, dc * 128:(dc + 1) * 128], identb[:, :]))
                    if dc == DC - 1:
                        t_utr[i] = s_t.n

                @block.vector
                def _(v, i=i, dc=dc):
                    wait(v, s_t)
                    s_v.inc(v.tensor_copy(uTsb[:, dc * TPC + i * 128: dc * TPC + (i + 1) * 128],
                                          psB[dc % 2][:, :]))
                    u_ev[dc] = s_v.n

            @block.tensor
            def _(t, i=i, bi=bi):
                wait(t, s_v, v_hilo)
                if i >= 2:
                    wait(t, s_v, v_lg[i - 2])
                for dc in range(DC):
                    t.matmul(psL[bi][:, :], hiT[:, dc * 128:(dc + 1) * 128], wgThi[:, dc, :],
                             start=(dc == 0), stop=False)
                    t.matmul(psL[bi][:, :], hiT[:, dc * 128:(dc + 1) * 128], wgTlo[:, dc, :],
                             start=False, stop=False)
                    inst = t.matmul(psL[bi][:, :], loT[:, dc * 128:(dc + 1) * 128], wgThi[:, dc, :],
                                    start=False, stop=(dc == DC - 1))
                s_t.inc(inst)
                t_lg[i] = s_t.n

            @block.vector
            def _(v, i=i, bi=bi):
                wait(v, s_t, t_lg[i])
                if i >= 1:
                    wait(v, s_gl, 16 * i)
                s_v.inc(v.tensor_copy(lgsb[:, :], psL[bi][:, :]))
                v_lg[i] = s_v.n

            @block.gpsimd
            def _(g, i=i):
                wait(g, s_v, v_lg[i])
                go_lg[i] = s_gl.inc(g.dma_start(out=o_lg[i * 128:(i + 1) * 128, :], in_=lgsb[:, :]), 16)

        @block.gpsimd
        def _(g):
            wait(g, s_v, s_v.n)
            for dc in range(DC):
                s_gu.inc(g.dma_start(out=o_uT[dc * 128:(dc + 1) * 128, :],
                                     in_=uTsb[:, dc * TPC:(dc + 1) * TPC]), 16)
            wait(g, s_gu)
            wait(g, s_gx)
            wait(g, s_gl)
    return nc


def _fresh_lists():
    return None


wg_ev = []
hilo_ev = [0] * DC
u_ev = [0] * DC
a_sq = [0] * NT
a_std = [0] * NT
a_n = [0] * NT


def _reset_state():
    global wg_ev, hilo_ev, u_ev, a_sq, a_std, a_n
    wg_ev = []
    hilo_ev = [0] * DC
    u_ev = [0] * DC
    a_sq = [0] * NT
    a_std = [0] * NT
    a_n = [0] * NT


_NC_CACHE = {}


def _get(name, builder):
    if name not in _NC_CACHE:
        _reset_state()
        _NC_CACHE[name] = builder()
    return _NC_CACHE[name]


def run_launchA(x, Wg):
    xf = np.ascontiguousarray(x.reshape(T, D))
    nc = _get("A", build_launchA)
    in_maps = [{"x_sh": xf[c * TPC:(c + 1) * TPC], "wg": Wg} for c in range(NC)]
    res = run_bass_kernel_spmd(nc, in_maps, core_ids=list(range(NC)))
    return res.results




NKT = S // 128
NQG = TPC // 256


def build_launchB():
    nc = bass.Bass("TRN2", target_bir_lowering=False, debug=False)
    uT_own = nc.dram_tensor("uT_own", [D, TPC], BF16, kind="ExternalInput").ap()
    xn_own = nc.dram_tensor("xn_own", [TPC, D], BF16, kind="ExternalInput").ap()
    keysT = nc.dram_tensor("keysT", [D, S], BF16, kind="ExternalInput").ap()
    keys_xn = nc.dram_tensor("keys_xn", [S, D], BF16, kind="ExternalInput").ap()
    o_s = nc.dram_tensor("o_s", [TPC, D], BF16, kind="ExternalOutput").ap()

    es = contextlib.ExitStack()
    with es:
        sb = lambda name, shape, dt: es.enter_context(nc.sbuf_tensor(name, shape, dt))
        psa = lambda name, shape, dt: es.enter_context(nc.psum_tensor(name, shape, dt))
        uTq = sb("uTq", [128, DC, TPC], BF16)
        uTk = [sb(f"uTk{j}", [128, DC, 128], BF16) for j in range(2)]
        xnk = [sb(f"xnk{j}", [128, D], BF16) for j in range(2)]
        PT = [sb(f"PT{j}", [128, 256], BF16) for j in range(2)]
        xq = [sb(f"xq{j}", [128, D], BF16) for j in range(2)]
        ssb = [sb(f"ssb{j}", [128, D], BF16) for j in range(2)]
        ones = sb("ones", [128, 1], BF16)
        neg1 = sb("neg1", [128, 1], F32)
        linv = sb("linv", [128, 2], F32)
        psS = [psa(f"psS{j}", [128, 256], F32) for j in range(2)]
        psO = [psa(f"psO{j}", [128, 512], F32) for j in range(4)]
        psl = [psa(f"psl{j}", [128, 1], F32) for j in range(2)]

        s_u = Sem(nc, es, "s_u")
        s_k = [Sem(nc, es, "s_k0"), Sem(nc, es, "s_k1")]
        s_q = [Sem(nc, es, "s_q0"), Sem(nc, es, "s_q1")]
        s_v = Sem(nc, es, "s_v")
        s_a = Sem(nc, es, "s_a")
        s_t = Sem(nc, es, "s_t")
        s_st = Sem(nc, es, "s_st")

        block = es.enter_context(nc.Block())

        @block.sync
        def _(sy):
            s_u.inc(sy.dma_start(
                out=uTq[:, :, :],
                in_=uT_own.rearrange("(c p) k -> p c k", p=128)), 16)

        @block.vector
        def _(v):
            s_v.inc(v.memset(ones[:, :], 1.0))
            s_v.inc(v.memset(neg1[:, :], -1.0))

        t_pv = [[0] * NKT for _ in range(NQG)]
        t_sc = [[0] * NKT for _ in range(NQG)]
        a_exp = [[0] * NKT for _ in range(NQG)]
        v_evac = [0] * NQG

        def emit_load(g, kt):
            ki = kt % 2

            @block.sync
            def _(sy, g=g, kt=kt, ki=ki):
                n = g * NKT + kt
                if n >= 2:
                    gp, kp = divmod(n - 2, NKT)
                    wait(sy, s_t, t_pv[gp][kp])
                s_k[ki].inc(sy.dma_start(
                    out=uTk[ki][:, :, :],
                    in_=keysT[:, kt * 128:(kt + 1) * 128].rearrange("(c p) k -> p c k", p=128)), 16)
                s_k[ki].inc(sy.dma_start(out=xnk[ki][:, :],
                                         in_=keys_xn[kt * 128:(kt + 1) * 128, :]), 16)

        def emit_scores(g, kt):
            ki = kt % 2

            @block.tensor
            def _(t, g=g, kt=kt, ki=ki):
                n = g * NKT + kt
                wait(t, s_k[ki], 32 * (n // 2 + 1))
                if n == 0:
                    wait(t, s_u, 16)
                if n >= 2:
                    gp, kp = divmod(n - 2, NKT)
                    wait(t, s_a, a_exp[gp][kp])
                for dc in range(DC):
                    inst = t.matmul(psS[ki][:, :], uTk[ki][:, dc, :],
                                    uTq[:, dc, g * 256:(g + 1) * 256],
                                    start=(dc == 0), stop=(dc == DC - 1))
                s_t.inc(inst)
                t_sc[g][kt] = s_t.n

        def emit_exp(g, kt):
            ki = kt % 2

            @block.scalar
            def _(a, g=g, kt=kt, ki=ki):
                if g == 0 and kt == 0:
                    wait(a, s_v, 2)
                wait(a, s_t, t_sc[g][kt])
                s_a.inc(a.activation(out=PT[ki][:, :], in_=psS[ki][:, :], func=AF.Exp,
                                     bias=neg1[:, 0:1]))
                a_exp[g][kt] = s_a.n

        def emit_pv(g, kt):
            ki = kt % 2

            @block.tensor
            def _(t, g=g, kt=kt, ki=ki):
                wait(t, s_a, a_exp[g][kt])
                if kt == 0:
                    if g >= 1:
                        wait(t, s_v, v_evac[g - 1])
                    else:
                        wait(t, s_v, 2)
                first = (kt == 0)
                last = (kt == NKT - 1)
                for sub in range(2):
                    for nh in range(2):
                        t.matmul(psO[sub * 2 + nh][:, :], PT[ki][:, sub * 128:(sub + 1) * 128],
                                 xnk[ki][:, nh * 512:(nh + 1) * 512], start=first, stop=last)
                    inst = t.matmul(psl[sub][:, :], PT[ki][:, sub * 128:(sub + 1) * 128],
                                    ones[:, :], start=first, stop=last)
                s_t.inc(inst)
                t_pv[g][kt] = s_t.n

        for g in range(NQG):
            for sub in range(2):
                @block.scalar
                def _(a, g=g, sub=sub):
                    if g >= 1:
                        wait(a, s_v, v_evac[g - 1])
                    s_q[sub].inc(a.dma_start(
                        out=xq[sub][:, :],
                        in_=xn_own[(g * 256 + sub * 128):(g * 256 + (sub + 1) * 128), :]), 16)

            emit_load(g, 0)
            emit_scores(g, 0)
            for kt in range(1, NKT):
                emit_load(g, kt)
                emit_exp(g, kt - 1)
                emit_scores(g, kt)
                emit_pv(g, kt - 1)
            emit_exp(g, NKT - 1)
            emit_pv(g, NKT - 1)

            @block.vector
            def _(v, g=g):
                wait(v, s_t, t_pv[g][NKT - 1])
                s_v.inc(v.reciprocal(linv[:, 0:1], psl[0][:, :]))
                s_v.inc(v.reciprocal(linv[:, 1:2], psl[1][:, :]))
                wait(v, s_v)
                for sub in range(2):
                    wait(v, s_q[sub], 16 * (g + 1))
                    if g >= 1:
                        wait(v, s_st, 32 * g)
                    for nh in range(2):
                        s_v.inc(v.scalar_tensor_tensor(
                            ssb[sub][:, nh * 512:(nh + 1) * 512], psO[sub * 2 + nh][:, :],
                            linv[:, sub:sub + 1], xq[sub][:, nh * 512:(nh + 1) * 512],
                            OP.mult, OP.add))
                v_evac[g] = s_v.n

            for sub in range(2):
                @block.gpsimd
                def _(gp_, g=g, sub=sub):
                    wait(gp_, s_v, v_evac[g])
                    s_st.inc(gp_.dma_start(
                        out=o_s[(g * 256 + sub * 128):(g * 256 + (sub + 1) * 128), :],
                        in_=ssb[sub][:, :]), 16)

        @block.gpsimd
        def _(gp_):
            wait(gp_, s_st)
    return nc


def run_launchB(uT_all, xn_all):
    nc = _get("B", build_launchB)
    in_maps = []
    for c in range(NC):
        b = c // 4
        keysT = np.concatenate([uT_all[4 * b + j] for j in range(4)], axis=1)
        in_maps.append({
            "uT_own": uT_all[c],
            "xn_own": xn_all[c * TPC:(c + 1) * TPC],
            "keysT": np.ascontiguousarray(keysT),
            "keys_xn": xn_all[b * S:(b + 1) * S],
        })
    res = run_bass_kernel_spmd(nc, in_maps, core_ids=list(range(NC)))
    return res.results




NTT = T // 128


def build_launchC():
    nc = bass.Bass("TRN2", target_bir_lowering=False, debug=False)
    lg_all = nc.dram_tensor("lg_all", [T, E0], F32, kind="ExternalInput").ap()
    metaf = nc.dram_tensor("metaf", [1, 8], F32, kind="ExternalInput").ap()
    metai = nc.dram_tensor("metai", [1, 8], I32, kind="ExternalInput").ap()
    o_list = nc.dram_tensor("o_list", [C0P, 2], U32, kind="ExternalOutput").ap()
    o_comb = nc.dram_tensor("o_comb", [TPC, 4], F32, kind="ExternalOutput").ap()

    es = contextlib.ExitStack()
    with es:
        sb = lambda name, shape, dt: es.enter_context(nc.sbuf_tensor(name, shape, dt))
        psa = lambda name, shape, dt: es.enter_context(nc.psum_tensor(name, shape, dt))
        io128 = sb("io128", [128, 128], F32)
        cid1 = sb("cid1", [128, 1], F32)
        identb = sb("identb", [128, 128], BF16)
        identf = sb("identf", [128, 128], F32)
        me = sb("me", [128, 1], F32)
        mcore = sb("mcore", [1, 8], I32)
        lgt = [sb(f"lgt{j}", [128, 4, E0], F32) for j in range(2)]
        st_ = sb("stC", [128, 8], F32)
        gates = sb("gates", [128, E0], F32)
        top8 = sb("top8", [128, 8], F32)
        tidx = sb("tidx", [128, 8], U32)
        tif = sb("tif", [128, 8], F32)
        ohtok = sb("ohtok", [128, NTT, 16], BF16)
        tw_sb = sb("tw_sb", [128, NTT, 2], F32)
        ti_sb = sb("ti_sb", [128, NTT, 2], F32)
        OH16 = sb("OH16", [16, T], BF16)
        CNT = sb("CNT", [16, T], F32)
        tot16 = sb("tot16", [16, 1], F32)
        shift16 = sb("shift16", [16, 16], F32)
        win_m = sb("win_m", [16, T], BF16)
        winT = sb("winT", [128, 16], BF16)
        poscT = sb("poscT", [128, 16], F32)
        tmp16 = sb("tmp16", [128, 16], F32)
        wp2 = sb("wp2", [128, 4], F32)
        m2 = sb("m2", [128, 2], F32)
        mw2 = sb("mw2", [128, 2], F32)
        idxf = sb("idxf", [128, 2], F32)
        idxu = sb("idxu", [128, 2], U32)
        vals = [sb(f"vals{j}", [128, 2], U32) for j in range(2)]
        tcol = sb("tcol", [128, 1], F32)
        hugeu = sb("hugeu", [128, 68], U32)
        comb_sb = sb("comb_sb", [128, NTT, 4], F32)
        psOH = psa("psOH", [16, 128], BF16)
        psW = [psa("psW0", [128, 16], BF16), psa("psW1", [128, 16], F32)]
        pst = psa("pst", [16, 1], F32)

        s_l = [Sem(nc, es, "s_l0"), Sem(nc, es, "s_l1")]
        s_m = Sem(nc, es, "s_m")
        s_v = Sem(nc, es, "s_v")
        s_a = Sem(nc, es, "s_a")
        s_t = Sem(nc, es, "s_t")
        s_g = Sem(nc, es, "s_g")
        s_sc = Sem(nc, es, "s_sc")
        s_st = Sem(nc, es, "s_st")
        brC = es.enter_context(nc.gpsimd.register("brC"))

        block = es.enter_context(nc.Block())

        @block.sync
        def _(sy):
            s_m.inc(sy.dma_start(out=me[:, :], in_=metaf[0:1, 0:1].partition_broadcast(128).squeeze(1)), 16)
            s_m.inc(sy.dma_start(out=mcore[:, :], in_=metai[:, :]), 16)

        @block.gpsimd
        def _(g):
            g.iota(io128[:, :], pattern=[[1, 128]], base=0, channel_multiplier=0,
                   allow_small_or_imprecise_dtypes=True)
            g.iota(cid1[:, :], pattern=[[0, 1]], base=0, channel_multiplier=1,
                   allow_small_or_imprecise_dtypes=True)
            s_g.inc(g.iota(hugeu[:, :], pattern=[[0, 68]], base=1000000000, channel_multiplier=0))
            g.reg_mov(brC, C0P - 1)
            wait(g, s_g)
            s_sc.inc(g.dma_start(out=o_list.rearrange("(p a) c -> p (a c)", p=128), in_=hugeu[:, :]), 16)

        @block.vector
        def _(v):
            wait(v, s_g)
            s_v.inc(v.tensor_scalar(identf[:, :], io128[:, :], cid1[:, :], None, OP.is_equal))
            wait(v, s_v)
            s_v.inc(v.tensor_copy(identb[:, :], identf[:, :]))
            wait(v, s_v)
            s_v.inc(v.tensor_scalar(shift16[0:16, :], io128[0:16, 0:16], cid1[0:16, :], 8.0,
                                    OP.subtract, OP.is_equal))
        v_const = s_v.n

        v_stat = [0] * NTT
        v_tile2 = [0] * NTT
        v_oh = [0] * NTT
        a_tile = [0] * NTT
        t_oh = [0] * NTT

        for j in range(NTT):
            lb2 = (j // 4) % 2
            if j % 4 == 0:
                @block.sync
                def _(sy, j=j, lb2=lb2):
                    if j >= 8:
                        wait(sy, s_a, a_tile[j - 8 + 3])
                    s_l[lb2].inc(sy.dma_start(
                        out=lgt[lb2][:, :, :],
                        in_=lg_all[j * 128:(j + 4) * 128, :].rearrange("(q p) e -> p q e", p=128)), 16)

            @block.vector
            def _(v, j=j, lb2=lb2):
                if j % 4 == 0:
                    wait(v, s_l[lb2], 16 * (j // 8 + 1))
                s_v.inc(v.reduce_max(st_[:, 0:1], lgt[lb2][:, j % 4, :], axis=AX.X))
                wait(v, s_v)
                s_v.inc(v.tensor_scalar(st_[:, 1:2], st_[:, 0:1], -1.0, None, OP.mult))
                v_stat[j] = s_v.n

            @block.scalar
            def _(a, j=j, lb2=lb2):
                wait(a, s_v, v_stat[j])
                if j >= 1:
                    wait(a, s_v, v_tile2[j - 1])
                s_a.inc(a.activation(out=gates[:, :], in_=lgt[lb2][:, j % 4, :], func=AF.Exp,
                                     bias=st_[:, 1:2], accum_out=st_[:, 2:3]))
                a_tile[j] = s_a.n

            @block.vector
            def _(v, j=j):
                wait(v, s_a, a_tile[j])
                s_v.inc(v.reciprocal(st_[:, 3:4], st_[:, 2:3]))
                wait(v, s_v)
                s_v.inc(v.tensor_scalar(gates[:, :], gates[:, :], st_[:, 3:4], None, OP.mult))
                wait(v, s_v)
                s_v.inc(v.max(top8[:, :], gates[:, :]))
                wait(v, s_v)
                s_v.inc(v.max_index(tidx[:, :], top8[:, :], gates[:, :]))
                wait(v, s_v)
                s_v.inc(v.tensor_copy(tif[:, 0:2], tidx[:, 0:2]))
                wait(v, s_v)
                s_v.inc(v.tensor_scalar(ohtok[:, j, 0:8], io128[:, 0:8], tif[:, 0:1], None, OP.is_equal))
                s_v.inc(v.tensor_scalar(ohtok[:, j, 8:16], io128[:, 0:8], tif[:, 1:2], None, OP.is_equal))
                s_v.inc(v.tensor_copy(tw_sb[:, j, :], top8[:, 0:2]))
                s_v.inc(v.tensor_copy(ti_sb[:, j, :], tif[:, 0:2]))
                v_tile2[j] = s_v.n

            @block.tensor
            def _(t, j=j):
                wait(t, s_v, v_tile2[j])
                if j == 0:
                    wait(t, s_v, v_const)
                if j >= 1:
                    wait(t, s_v, v_oh[j - 1])
                s_t.inc(t.transpose(psOH[:, :], ohtok[:, j, :], identb[:, :]))
                t_oh[j] = s_t.n

            @block.vector
            def _(v, j=j):
                wait(v, s_t, t_oh[j])
                s_v.inc(v.tensor_copy(OH16[:, j * 128:(j + 1) * 128], psOH[0:16, :]))
                v_oh[j] = s_v.n

        @block.vector
        def _(v):
            wait(v, s_v)
            s_v.inc(v.tensor_tensor_scan(CNT[:, :], OH16[:, :], OH16[:, :], 0.0, OP.add, OP.bypass))
        v_scan = s_v.n

        @block.tensor
        def _(t):
            wait(t, s_v, v_scan)
            s_t.inc(t.matmul(pst[:, :], shift16[0:16, :], CNT[:, T - 1:T], start=True, stop=True))
        t_tot = s_t.n

        @block.vector
        def _(v):
            wait(v, s_t, t_tot)
            s_v.inc(v.tensor_copy(tot16[:, :], pst[:, :]))
            wait(v, s_v)
            s_v.inc(v.tensor_scalar(CNT[:, :], CNT[:, :], tot16[:, 0:1], -1.0, OP.add, OP.add))
            wait(v, s_v)
            s_v.inc(v.tensor_scalar(win_m[:, :], CNT[:, :], float(C0), None, OP.is_lt))
            wait(v, s_v)
            s_v.inc(v.tensor_scalar(CNT[:, :], CNT[:, :], float(C0 - 1), None, OP.min))
        v_mats = s_v.n

        t_ext = [0] * NTT
        v_ext = [0] * NTT
        for j in range(NTT):
            @block.tensor
            def _(t, j=j):
                if j == 0:
                    wait(t, s_v, v_mats)
                if j >= 1:
                    wait(t, s_v, v_ext[j - 1])
                s_t.inc(t.transpose(psW[0][:, :], win_m[0:16, j * 128:(j + 1) * 128], identb[0:16, 0:16]))
                s_t.inc(t.transpose(psW[1][:, :], CNT[0:16, j * 128:(j + 1) * 128], identf[0:16, 0:16]))
                t_ext[j] = s_t.n

            @block.vector
            def _(v, j=j):
                wait(v, s_t, t_ext[j])
                if j == 0:
                    wait(v, s_m, 32)
                if j >= 1:
                    wait(v, s_sc, 16 * (1 + 2 * j))
                s_v.inc(v.tensor_copy(winT[:, :], psW[0][:, :]))
                s_v.inc(v.tensor_copy(poscT[:, :], psW[1][:, :]))
                wait(v, s_v)
                s_v.inc(v.tensor_tensor(tmp16[:, :], ohtok[:, j, :], winT[:, :], OP.mult))
                wait(v, s_v)
                s_v.inc(v.tensor_reduce(wp2[:, 0:2], tmp16.rearrange("p (k e) -> p k e", k=2), AX.X, OP.add))
                wait(v, s_v)
                s_v.inc(v.tensor_tensor(tmp16[:, :], ohtok[:, j, :], poscT[:, :], OP.mult))
                wait(v, s_v)
                s_v.inc(v.tensor_reduce(wp2[:, 2:4], tmp16.rearrange("p (k e) -> p k e", k=2), AX.X, OP.add))
                wait(v, s_v)
                s_v.inc(v.tensor_tensor(comb_sb[:, j, 0:2], tw_sb[:, j, :], wp2[:, 0:2], OP.mult))
                s_v.inc(v.tensor_scalar(idxf[:, :], ti_sb[:, j, :], float(C0P), None, OP.mult))
                wait(v, s_v)
                s_v.inc(v.tensor_tensor(comb_sb[:, j, 2:4], idxf[:, :], wp2[:, 2:4], OP.add))
                s_v.inc(v.tensor_scalar(m2[:, :], ti_sb[:, j, :], me[:, 0:1], None, OP.is_equal))
                wait(v, s_v)
                s_v.inc(v.tensor_tensor(mw2[:, :], m2[:, :], wp2[:, 0:2], OP.mult))
                s_v.inc(v.tensor_scalar(idxf[:, :], wp2[:, 2:4], 100000.0, None, OP.subtract))
                wait(v, s_v)
                s_v.inc(v.tensor_tensor(idxf[:, :], idxf[:, :], mw2[:, :], OP.mult))
                wait(v, s_v)
                s_v.inc(v.tensor_scalar(idxf[:, :], idxf[:, :], 100000.0, None, OP.add))
                wait(v, s_v)
                s_v.inc(v.tensor_copy(idxu[:, :], idxf[:, :]))
                s_v.inc(v.tensor_scalar(tcol[:, :], cid1[:, :], float(j * 128), None, OP.add))
                wait(v, s_v)
                s_v.inc(v.tensor_copy(vals[0][:, 0:1], tcol[:, :]))
                s_v.inc(v.tensor_copy(vals[1][:, 0:1], tcol[:, :]))
                s_v.inc(v.tensor_copy(vals[0][:, 1:2].bitcast(F32), comb_sb[:, j, 0:1]))
                s_v.inc(v.tensor_copy(vals[1][:, 1:2].bitcast(F32), comb_sb[:, j, 1:2]))
                v_ext[j] = s_v.n

            @block.gpsimd
            def _(g, j=j):
                wait(g, s_v, v_ext[j])
                if j == 0:
                    wait(g, s_sc, 16)
                s_sc.inc(g.indirect_dma_start(
                    out=o_list, out_offset=IndirectOffsetOnAxis(ap=idxu[:, 0:1], axis=0),
                    in_=vals[0][:, :], in_offset=None,
                    bounds_check=brC, oob_is_err=False), 16)
                s_sc.inc(g.indirect_dma_start(
                    out=o_list, out_offset=IndirectOffsetOnAxis(ap=idxu[:, 1:2], axis=0),
                    in_=vals[1][:, :], in_offset=None,
                    bounds_check=brC, oob_is_err=False), 16)

        @block.gpsimd
        def _(g):
            wait(g, s_v, s_v.n)
            wait(g, s_m, 32)
            with g.register("swr") as swr:
                g.reg_load(swr, mcore[0:1, 1:2])
                for c in nc.Switch(engines=[g], index=[RuntimeValue(swr)], n=8):
                    s_st.inc(g.dma_start(
                        out=o_comb.rearrange("(tl p) c -> p tl c", p=128),
                        in_=comb_sb[:, c * NT:(c + 1) * NT, :]), 16)
                    s_st.n -= 16
            s_st.n += 16
            wait(g, s_st)
            wait(g, s_sc)
    return nc


def run_launchC(lg_all):
    nc = _get("C", build_launchC)
    in_maps = []
    for c in range(NC):
        mf = np.zeros((1, 8), np.float32); mf[0, 0] = c
        mi = np.zeros((1, 8), np.int32); mi[0, 1] = c
        mi[0, 0] = c
        in_maps.append({"lg_all": lg_all, "metaf": mf, "metai": mi})
    res = run_bass_kernel_spmd(nc, in_maps, core_ids=list(range(NC)))
    return res.results




H0 = 4096
HC0 = H0 // 128
NDT = C0P // 256


def build_launchD():
    nc = bass.Bass("TRN2", target_bir_lowering=False, debug=False)
    lst = nc.dram_tensor("lst", [C0P, 2], U32, kind="ExternalInput").ap()
    xn_tbl = nc.dram_tensor("xn_tbl", [T, D], BF16, kind="ExternalInput").ap()
    w1 = nc.dram_tensor("w1", [D, H0], F32, kind="ExternalInput").ap()
    b1v = nc.dram_tensor("b1v", [1, H0], F32, kind="ExternalInput").ap()
    w2 = nc.dram_tensor("w2", [H0, D], F32, kind="ExternalInput").ap()
    b2v = nc.dram_tensor("b2v", [1, D], F32, kind="ExternalInput").ap()
    o_y = nc.dram_tensor("o_y", [C0P, D], BF16, kind="ExternalOutput").ap()

    es = contextlib.ExitStack()
    with es:
        sb = lambda name, shape, dt: es.enter_context(nc.sbuf_tensor(name, shape, dt))
        psa = lambda name, shape, dt: es.enter_context(nc.psum_tensor(name, shape, dt))
        io128 = sb("io128", [128, 128], F32)
        cid1 = sb("cid1", [128, 1], F32)
        identb = sb("identb", [128, 128], BF16)
        identf = sb("identf", [128, 128], F32)
        w1sb = sb("w1sb", [128, DC, H0], BF16)
        w2sb = sb("w2sb", [128, HC0, D], BF16)
        b1r = sb("b1r", [HC0, 128], F32)
        b1T = sb("b1T", [128, HC0], F32)
        b2r = sb("b2r", [DC, 128], F32)
        b2T = sb("b2T", [128, DC], F32)
        lidu = [sb(f"lidu{j}", [128, 2, 2], U32) for j in range(2)]
        xg = [sb(f"xg{j}", [128, D], BF16) for j in range(2)]
        xT = [sb(f"xT{j}", [128, DC, 256], BF16) for j in range(2)]
        hT = sb("hT", [128, HC0, 256], BF16)
        yT = sb("yT", [128, DC, 256], BF16)
        yout = sb("yout", [128, 2, D], BF16)
        psH = [psa(f"psH{j}", [128, 256], F32) for j in range(2)]
        psY = [psa(f"psY{j}", [128, 256], F32) for j in range(2)]
        psB = [psa(f"psB{j}", [128, 128], BF16) for j in range(2)]
        psF = psa("psF", [128, 128], F32)

        s_w = Sem(nc, es, "s_w")
        s_b = Sem(nc, es, "s_b")
        s_li = [Sem(nc, es, "s_li0"), Sem(nc, es, "s_li1")]
        s_gx = [Sem(nc, es, "s_gx0"), Sem(nc, es, "s_gx1")]
        s_v = Sem(nc, es, "s_v")
        s_a = Sem(nc, es, "s_a")
        s_t = Sem(nc, es, "s_t")
        s_g = Sem(nc, es, "s_g")
        s_ga = Sem(nc, es, "s_ga")
        s_st = Sem(nc, es, "s_st")
        brD = es.enter_context(nc.gpsimd.register("brD"))

        block = es.enter_context(nc.Block())

        @block.gpsimd
        def _(g):
            g.iota(io128[:, :], pattern=[[1, 128]], base=0, channel_multiplier=0,
                   allow_small_or_imprecise_dtypes=True)
            s_g.inc(g.iota(cid1[:, :], pattern=[[0, 1]], base=0, channel_multiplier=1,
                           allow_small_or_imprecise_dtypes=True))
            g.reg_mov(brD, T - 1)
            s_w.inc(g.dma_start(out=w1sb[:, :, :], in_=w1.rearrange("(dc p) h -> p dc h", p=128)), 16)
            s_w.inc(g.dma_start(out=w2sb[:, :, :], in_=w2.rearrange("(hc p) d -> p hc d", p=128)), 16)

        @block.sync
        def _(sy):
            s_b.inc(sy.dma_start(out=b1r[:, :], in_=b1v.rearrange("o (hc p) -> (o hc) p", p=128)), 16)
            s_b.inc(sy.dma_start(out=b2r[:, :], in_=b2v.rearrange("o (dc p) -> (o dc) p", p=128)), 16)

        @block.vector
        def _(v):
            wait(v, s_g)
            s_v.inc(v.tensor_scalar(identf[:, :], io128[:, :], cid1[:, :], None, OP.is_equal))
            wait(v, s_v)
            s_v.inc(v.tensor_copy(identb[:, :], identf[:, :]))
            s_v.inc(v.memset(xg[0][:, :], 0.0))
            s_v.inc(v.memset(xg[1][:, :], 0.0))
        v_id = s_v.n

        @block.tensor
        def _(t):
            wait(t, s_v, v_id)
            wait(t, s_w, 32)
            wait(t, s_b, 32)
            s_t.inc(t.transpose(psF[:, 0:HC0], b1r[:, :], identf[0:HC0, 0:HC0]))
            s_t.inc(t.transpose(psH[0][:, 0:DC], b2r[:, :], identf[0:DC, 0:DC]))
        t_b = s_t.n

        @block.vector
        def _(v):
            wait(v, s_t, t_b)
            s_v.inc(v.tensor_copy(b1T[:, :], psF[:, 0:HC0]))
            s_v.inc(v.tensor_copy(b2T[:, :], psH[0][:, 0:DC]))
        v_b = s_v.n

        tout_prev = [0]
        t_x = [0] * NDT
        t_h = [0] * NDT
        t_y = [0] * NDT
        v_tr = [0] * NDT
        v_yev = [0] * NDT
        a_g = [0] * NDT
        a_y = [0] * NDT
        g_done = [0] * NDT

        for it in range(NDT):
            li = it % 2

            @block.scalar
            def _(a, it=it, li=li):
                if it >= 2:
                    wait(a, s_ga, 32 * (it - 1))
                s_li[li].inc(a.dma_start(
                    out=lidu[li][:, :, :],
                    in_=lst[it * 256:(it + 1) * 256, :].rearrange("(s p) c -> p s c", p=128)), 16)

            @block.gpsimd
            def _(g, it=it, li=li):
                wait(g, s_li[li], 16 * (it // 2 + 1))
                if it >= 1:
                    wait(g, s_t, t_x[it - 1])
                if it == 0:
                    wait(g, s_v, v_id)
                for sub in range(2):
                    s_ga.inc(g.indirect_dma_start(
                        out=xg[sub][:, :], out_offset=None,
                        in_=xn_tbl, in_offset=IndirectOffsetOnAxis(ap=lidu[li][:, sub, 0:1], axis=0),
                        bounds_check=brD, oob_is_err=False), 16)

            @block.tensor
            def _(t, it=it):
                wait(t, s_ga, 32 * (it + 1))
                if it >= 1:
                    wait(t, s_v, v_yev[it - 1])
                vbase = s_v.n + (1 if it == 0 else 0)
                vbase = s_v.n
                for k in range(16):
                    sub, dc = divmod(k, DC)
                    if k >= 2:
                        wait(t, s_v, vbase + k - 1)
                    s_t.inc(t.transpose(psB[dc % 2][:, :], xg[sub][:, dc * 128:(dc + 1) * 128],
                                        identb[:, :]))
                t_x[it] = s_t.n

            @block.vector
            def _(v, it=it):
                base = t_x[it] - 16
                for k in range(16):
                    sub, dc = divmod(k, DC)
                    wait(v, s_t, base + k + 1)
                    s_v.inc(v.tensor_copy(xT[it % 2][:, dc, sub * 128:(sub + 1) * 128],
                                          psB[dc % 2][:, :]))
                v_tr[it] = s_v.n

            @block.tensor
            def _(t, it=it):
                wait(t, s_v, v_tr[it])
                abase = s_a.n
                for hc in range(HC0):
                    if hc >= 2:
                        wait(t, s_a, abase + hc - 1)
                    for dc in range(DC):
                        inst = t.matmul(psH[hc % 2][:, :], w1sb[:, dc, hc * 128:(hc + 1) * 128],
                                        xT[it % 2][:, dc, :], start=(dc == 0), stop=(dc == DC - 1))
                    s_t.inc(inst)
                t_h[it] = s_t.n

            @block.scalar
            def _(a, it=it):
                base = t_h[it] - HC0
                if it == 0:
                    wait(a, s_v, v_b)
                if it >= 1:
                    wait(a, s_t, t_y[it - 1])
                for hc in range(HC0):
                    wait(a, s_t, base + hc + 1)
                    s_a.inc(a.activation(out=hT[:, hc, :], in_=psH[hc % 2][:, :], func=AF.Gelu,
                                         bias=b1T[:, hc:hc + 1]))
                a_g[it] = s_a.n

            @block.tensor
            def _(t, it=it):
                ybase = s_a.n
                wait(t, s_a, a_g[it])
                for dc2 in range(DC):
                    if dc2 >= 2:
                        wait(t, s_a, ybase + dc2 - 1)
                    for hc in range(HC0):
                        inst = t.matmul(psY[dc2 % 2][:, :], w2sb[:, hc, dc2 * 128:(dc2 + 1) * 128],
                                        hT[:, hc, :], start=(hc == 0), stop=(hc == HC0 - 1))
                    s_t.inc(inst)
                t_y[it] = s_t.n

            @block.scalar
            def _(a, it=it):
                base = t_y[it] - DC
                if it >= 1:
                    wait(a, s_t, tout_prev[0])
                for dc2 in range(DC):
                    wait(a, s_t, base + dc2 + 1)
                    s_a.inc(a.activation(out=yT[:, dc2, :], in_=psY[dc2 % 2][:, :], func=AF.Identity,
                                         bias=b2T[:, dc2:dc2 + 1]))
                a_y[it] = s_a.n

            @block.tensor
            def _(t, it=it):
                wait(t, s_a, a_y[it])
                vbase2 = s_v.n
                for k in range(16):
                    sub, dc2 = divmod(k, DC)
                    if k >= 2:
                        wait(t, s_v, vbase2 + k - 1)
                    s_t.inc(t.transpose(psB[dc2 % 2][:, :],
                                        yT[:, dc2, sub * 128:(sub + 1) * 128], identb[:, :]))
                tout_prev[0] = s_t.n

            @block.vector
            def _(v, it=it):
                base = tout_prev[0] - 16
                if it >= 1:
                    wait(v, s_st, 16 * it)
                for k in range(16):
                    sub, dc2 = divmod(k, DC)
                    wait(v, s_t, base + k + 1)
                    s_v.inc(v.tensor_copy(yout[:, sub, dc2 * 128:(dc2 + 1) * 128],
                                          psB[dc2 % 2][:, :]))
                v_yev[it] = s_v.n

            @block.gpsimd
            def _(g, it=it):
                wait(g, s_v, v_yev[it])
                s_st.inc(g.dma_start(
                    out=o_y[it * 256:(it + 1) * 256, :].rearrange("(s p) d -> p s d", p=128),
                    in_=yout[:, :, :]), 16)

        @block.gpsimd
        def _(g):
            wait(g, s_st)
    return nc


def run_launchD(lists, xn_all, W1, b1, W2, b2):
    nc = _get("D", build_launchD)
    in_maps = []
    for c in range(NC):
        in_maps.append({
            "lst": lists[c], "xn_tbl": xn_all,
            "w1": W1[c], "b1v": b1[c:c + 1], "w2": W2[c], "b2v": b2[c:c + 1],
        })
    res = run_bass_kernel_spmd(nc, in_maps, core_ids=list(range(NC)))
    return res.results






QO = 3072
QOC = QO // 128


def build_launchE():
    nc = bass.Bass("TRN2", target_bir_lowering=False, debug=False)
    x_sh = nc.dram_tensor("x_sh", [TPC, D], F32, kind="ExternalInput").ap()
    s_bf = nc.dram_tensor("s_bf", [TPC, D], BF16, kind="ExternalInput").ap()
    y_tbl = nc.dram_tensor("y_tbl", [NC * C0P, D], BF16, kind="ExternalInput").ap()
    comb = nc.dram_tensor("comb", [TPC, 4], F32, kind="ExternalInput").ap()
    ipw = nc.dram_tensor("ipw", [QO, D], F32, kind="ExternalInput").ap()
    ipb = nc.dram_tensor("ipb", [1, QO], F32, kind="ExternalInput").ap()
    gw1 = nc.dram_tensor("gw1", [D, E1], F32, kind="ExternalInput").ap()
    o_x1 = nc.dram_tensor("o_x1", [TPC, D], F32, kind="ExternalOutput").ap()
    o_xn1 = nc.dram_tensor("o_xn1", [TPC, D], BF16, kind="ExternalOutput").ap()
    o_xn1T = nc.dram_tensor("o_xn1T", [D, TPC], BF16, kind="ExternalOutput").ap()
    o_qkv = nc.dram_tensor("o_qkv", [TPC, QO], BF16, kind="ExternalOutput").ap()
    o_lg1 = nc.dram_tensor("o_lg1", [TPC, E1], F32, kind="ExternalOutput").ap()

    es = contextlib.ExitStack()
    with es:
        sb = lambda name, shape, dt: es.enter_context(nc.sbuf_tensor(name, shape, dt))
        psa = lambda name, shape, dt: es.enter_context(nc.psum_tensor(name, shape, dt))
        io128 = sb("io128", [128, 128], F32)
        cid1 = sb("cid1", [128, 1], F32)
        identf = sb("identf", [128, 128], F32)
        identb = sb("identb", [128, 128], BF16)
        epsc = sb("epsc", [128, 1], F32)
        ones1 = sb("ones1", [1, 128], BF16)
        ipwn = sb("ipwn", [128, QOC, D], BF16)
        ipwT = sb("ipwT", [128, DC, QO], BF16)
        ipbr = sb("ipbr", [1, QO], BF16)
        gwhi = sb("gwhi", [128, DC, E1], BF16)
        gwlo = sb("gwlo", [128, DC, E1], BF16)
        gwf = sb("gwf", [128, DC, E1], F32)
        combt = [sb(f"combt{j}", [128, 4], F32) for j in range(2)]
        giu = sb("giu", [128, 2], U32)
        yg = [sb(f"yg{j}", [128, D], BF16) for j in range(2)]
        xt = [sb(f"xt{j}", [128, D], F32) for j in range(2)]
        st2 = [sb(f"st2{j}", [128, D], BF16) for j in range(2)]
        gpre = sb("gpre", [128, D], F32)
        x1f = [sb(f"x1f{j}", [128, D], F32) for j in range(2)]
        xc = sb("xc", [128, D], F32)
        xnf = [sb(f"xnf{j}", [128, D], F32) for j in range(2)]
        sq = sb("sq", [128, D], F32)
        xnbf = [sb(f"xnbf{j}", [128, D], BF16) for j in range(2)]
        lobf = sb("lobf", [128, D], BF16)
        st_ = sb("stats", [128, 8], F32)
        hiT = sb("hiT", [128, DC * 128], BF16)
        loT = sb("loT", [128, DC * 128], BF16)
        qkvsb = [sb(f"qkvsb{j}", [128, QO], BF16) for j in range(2)]
        lgsb = [sb(f"lgsb{j}", [128, E1], F32) for j in range(2)]
        psT = [psa(f"psT{j}", [128, 128], F32) for j in range(4)]
        psQ = [psa(f"psQ{j}", [128, 512], F32) for j in range(2)]
        psL = [psa(f"psL{j}", [128, E1], F32) for j in range(2)]

        s_x = [Sem(nc, es, "s_x0"), Sem(nc, es, "s_x1")]
        s_w = Sem(nc, es, "s_w")
        s_ga = [Sem(nc, es, "s_ga0"), Sem(nc, es, "s_ga1")]
        s_v = Sem(nc, es, "s_v")
        s_a = Sem(nc, es, "s_a")
        s_t = Sem(nc, es, "s_t")
        s_g = Sem(nc, es, "s_g")
        s_s1 = Sem(nc, es, "s_s1")
        s_s2 = Sem(nc, es, "s_s2")
        s_s3 = Sem(nc, es, "s_s3")
        s_s4 = Sem(nc, es, "s_s4")
        s_s5 = Sem(nc, es, "s_s5")
        brE = es.enter_context(nc.gpsimd.register("brE"))

        block = es.enter_context(nc.Block())
        uTsb = sb("uTsbE", [128, DC * TPC], BF16)

        @block.gpsimd
        def _(g):
            g.iota(io128[:, :], pattern=[[1, 128]], base=0, channel_multiplier=0,
                   allow_small_or_imprecise_dtypes=True)
            s_g.inc(g.iota(cid1[:, :], pattern=[[0, 1]], base=0, channel_multiplier=1,
                           allow_small_or_imprecise_dtypes=True))
            g.reg_mov(brE, NC * C0P - 1)
            s_w.inc(g.dma_start(out=ipwn[:, :, :], in_=ipw.rearrange("(oc p) d -> p oc d", p=128)), 16)
            s_w.inc(g.dma_start(out=ipbr[:, :], in_=ipb[:, :]), 16)
            s_w.inc(g.dma_start(out=gwf[:, :, :], in_=gw1.rearrange("(dc p) e -> p dc e", p=128)), 16)

        @block.vector
        def _(v):
            s_v.inc(v.memset(epsc[:, :], 1e-5))
            s_v.inc(v.memset(ones1[:, :], 1.0))
            s_v.inc(v.memset(yg[0][:, :], 0.0))
            s_v.inc(v.memset(yg[1][:, :], 0.0))
            wait(v, s_g)
            s_v.inc(v.tensor_scalar(identf[:, :], io128[:, :], cid1[:, :], None, OP.is_equal))
            wait(v, s_v)
            s_v.inc(v.tensor_copy(identb[:, :], identf[:, :]))
            wait(v, s_w, 48)
            s_v.inc(v.tensor_copy(gwhi[:, :, :], gwf[:, :, :]))
            wait(v, s_v)
            s_v.inc(v.tensor_tensor(gwlo[:, :, :], gwf[:, :, :], gwhi[:, :, :], OP.subtract))
        v_pre = s_v.n

        tw_ev = [0] * (QOC * DC)
        for k in range(QOC * DC):
            oc, dc = divmod(k, DC)

            @block.tensor
            def _(t, k=k, oc=oc, dc=dc):
                if k == 0:
                    wait(t, s_v, v_pre)
                if k >= 4:
                    wait(t, s_v, tw_ev[k - 4])
                s_t.inc(t.transpose(psT[k % 4][:, :].bitcast(BF16)[:, 0:128],
                                    ipwn[:, oc, dc * 128:(dc + 1) * 128], identb[:, :]))

            @block.vector
            def _(v, k=k, oc=oc, dc=dc):
                wait(v, s_t)
                s_v.inc(v.tensor_copy(ipwT[:, dc, oc * 128:(oc + 1) * 128],
                                      psT[k % 4][:, :].bitcast(BF16)[:, 0:128]))
                tw_ev[k] = s_v.n
        v_wT = s_v.n

        v_comb = [0] * NT
        v_gp = [0] * NT
        hl_vbase = [0]
        v_x1 = [0] * NT
        v_xc = [0] * NT
        v_xn = [0] * NT
        v_fin = [0] * NT
        v_hilo = [0] * NT
        v_qkv = [0] * NT
        v_lg = [0] * NT
        a_sq = [0] * NT
        a_std = [0] * NT
        a_sil = [0] * NT
        t_lg = [0] * NT
        t_qkv = [0] * NT
        ga_n = [0] * NT

        for i in range(NT):
            bi = i % 2

            @block.sync
            def _(sy, i=i, bi=bi):
                if i >= 2:
                    wait(sy, s_v, v_x1[i - 2])
                s_x[bi].inc(sy.dma_start(out=combt[bi][:, :], in_=comb[i * 128:(i + 1) * 128, :]), 16)
                s_x[bi].inc(sy.dma_start(out=xt[bi][:, :], in_=x_sh[i * 128:(i + 1) * 128, :]), 16)
                s_x[bi].inc(sy.dma_start(out=st2[bi][:, :], in_=s_bf[i * 128:(i + 1) * 128, :]), 16)

            @block.vector
            def _(v, i=i, bi=bi):
                wait(v, s_x[bi], 48 * (i // 2 + 1))
                if i >= 1:
                    wait(v, s_ga[(i - 1) % 2], 32 * ((i - 1) // 2 + 1))
                s_v.inc(v.tensor_copy(giu[:, :], combt[bi][:, 2:4]))
                v_comb[i] = s_v.n

            @block.gpsimd
            def _(g, i=i, bi=bi):
                wait(g, s_v, v_comb[i])
                if i >= 1:
                    wait(g, s_v, v_x1[i - 1])
                if i == 0:
                    wait(g, s_v, 4)
                for sub in range(2):
                    s_ga[bi].inc(g.indirect_dma_start(
                        out=yg[sub][:, :], out_offset=None,
                        in_=y_tbl, in_offset=IndirectOffsetOnAxis(ap=giu[:, sub:sub + 1], axis=0),
                        bounds_check=brE, oob_is_err=False), 16)
                ga_n[i] = s_ga[bi].n

            @block.vector
            def _(v, i=i, bi=bi):
                wait(v, s_ga[bi], ga_n[i])
                s_v.inc(v.tensor_scalar(gpre[:, :], yg[0][:, :], combt[bi][:, 0:1], None, OP.mult))
                wait(v, s_v)
                s_v.inc(v.scalar_tensor_tensor(gpre[:, :], yg[1][:, :], combt[bi][:, 1:2],
                                               gpre[:, :], OP.mult, OP.add))
                v_gp[i] = s_v.n

            @block.scalar
            def _(a, i=i, bi=bi):
                wait(a, s_v, v_gp[i])
                s_a.inc(a.activation(out=gpre[:, :], in_=gpre[:, :], func=AF.Silu))
                a_sil[i] = s_a.n

            @block.vector
            def _(v, i=i, bi=bi):
                wait(v, s_a, a_sil[i])
                s_v.inc(v.tensor_tensor(gpre[:, :], gpre[:, :], st2[bi][:, :], OP.mult))
                wait(v, s_v)
                if i >= 2:
                    wait(v, s_s1, 16 * i)
                s_v.inc(v.tensor_tensor(x1f[bi][:, :], gpre[:, :], xt[bi][:, :], OP.add))
                v_x1[i] = s_v.n

            @block.gpsimd
            def _(g, i=i, bi=bi):
                wait(g, s_v, v_x1[i])
                s_s1.inc(g.dma_start(out=o_x1[i * 128:(i + 1) * 128, :], in_=x1f[bi][:, :]), 16)

            @block.vector
            def _(v, i=i, bi=bi):
                wait(v, s_v)
                s_v.inc(v.tensor_reduce(st_[:, 0:1], x1f[bi][:, :], AX.X, OP.add))
                wait(v, s_v)
                s_v.inc(v.tensor_scalar(st_[:, 1:2], st_[:, 0:1], 1.0 / D, None, OP.mult))
                wait(v, s_v)
                if i >= 1:
                    wait(v, s_a, a_sq[i - 1])
                s_v.inc(v.tensor_scalar(xc[:, :], x1f[bi][:, :], st_[:, 1:2], None, OP.subtract))
                v_xc[i] = s_v.n

            @block.scalar
            def _(a, i=i):
                wait(a, s_v, v_xc[i])
                s_a.inc(a.activation(out=sq[:, :], in_=xc[:, :], func=AF.Square, accum_out=st_[:, 2:3]))
                a_sq[i] = s_a.n

            @block.vector
            def _(v, i=i):
                wait(v, s_a, a_sq[i])
                s_v.inc(v.tensor_scalar(st_[:, 3:4], st_[:, 2:3], 1.0 / D, None, OP.mult))

            @block.scalar
            def _(a, i=i):
                wait(a, s_v)
                s_a.inc(a.activation(out=st_[:, 4:5], in_=st_[:, 3:4], func=AF.Sqrt, bias=epsc[:, 0:1]))
                a_std[i] = s_a.n

            @block.vector
            def _(v, i=i, bi=bi):
                wait(v, s_a, a_std[i])
                s_v.inc(v.reciprocal(st_[:, 5:6], st_[:, 4:5]))
                wait(v, s_v)
                if i >= 2:
                    wait(v, s_t, t_lg[i - 2])
                s_v.inc(v.tensor_scalar(xnf[bi][:, :], xc[:, :], st_[:, 5:6], None, OP.mult))
                wait(v, s_v)
                if i >= 2:
                    wait(v, s_s2, 16 * i)
                s_v.inc(v.tensor_copy(xnbf[bi][:, :], xnf[bi][:, :]))
                wait(v, s_v)
                s_v.inc(v.tensor_tensor(lobf[:, :], xnf[bi][:, :], xnbf[bi][:, :], OP.subtract))
                v_fin[i] = s_v.n

            @block.gpsimd
            def _(g, i=i, bi=bi):
                wait(g, s_v, v_fin[i])
                s_s2.inc(g.dma_start(out=o_xn1[i * 128:(i + 1) * 128, :], in_=xnbf[bi][:, :]), 16)

            for dc in range(DC):
                @block.tensor
                def _(t, i=i, bi=bi, dc=dc):
                    if dc == 0:
                        wait(t, s_v, v_fin[i])
                        if i == 0:
                            wait(t, s_v, v_wT)
                    vb = hl_vbase[0]
                    if dc >= 2:
                        wait(t, s_v, vb + 3 * (dc - 2) + 3)
                    s_t.inc(t.transpose(psT[dc % 2][:, :], xnf[bi][:, dc * 128:(dc + 1) * 128],
                                        identf[:, :]))
                    s_t.inc(t.transpose(psT[2 + dc % 2][:, :].bitcast(BF16)[:, 0:128],
                                        lobf[:, dc * 128:(dc + 1) * 128], identb[:, :]))

                @block.vector
                def _(v, i=i, dc=dc):
                    if dc == 0:
                        hl_vbase[0] = s_v.n
                    wait(v, s_t)
                    s_v.inc(v.tensor_copy(hiT[:, dc * 128:(dc + 1) * 128], psT[dc % 2][:, :]))
                    wait(v, s_v)
                    s_v.inc(v.tensor_copy(uTsb[:, dc * TPC + i * 128:dc * TPC + (i + 1) * 128],
                                          hiT[:, dc * 128:(dc + 1) * 128]))
                    wait(v, s_v)
                    s_v.inc(v.tensor_copy(loT[:, dc * 128:(dc + 1) * 128],
                                          psT[2 + dc % 2][:, :].bitcast(BF16)[:, 0:128]))
                v_hilo[i] = s_v.n

            @block.tensor
            def _(t, i=i, bi=bi):
                wait(t, s_v, v_hilo[i])
                qbase = s_a.n
                for oq in range(6):
                    if oq >= 2:
                        wait(t, s_a, qbase + oq - 1)
                    t.matmul(psQ[oq % 2][:, :], ones1[:, :], ipbr[:, oq * 512:(oq + 1) * 512],
                             start=True, stop=False)
                    for dc in range(DC):
                        inst = t.matmul(psQ[oq % 2][:, :], hiT[:, dc * 128:(dc + 1) * 128],
                                        ipwT[:, dc, oq * 512:(oq + 1) * 512],
                                        start=False, stop=(dc == DC - 1))
                    s_t.inc(inst)
                t_qkv[i] = s_t.n

            @block.scalar
            def _(a, i=i, bi=bi):
                base = t_qkv[i] - 6
                if i >= 2:
                    wait(a, s_s3, 16 * i)
                for oq in range(6):
                    wait(a, s_t, base + oq + 1)
                    s_a.inc(a.activation(out=qkvsb[bi][:, oq * 512:(oq + 1) * 512],
                                         in_=psQ[oq % 2][:, :], func=AF.Identity))
                a_qkv = s_a.n

            @block.gpsimd
            def _(g, i=i, bi=bi):
                wait(g, s_a, s_a.n)
                s_s3.inc(g.dma_start(out=o_qkv[i * 128:(i + 1) * 128, :], in_=qkvsb[bi][:, :]), 16)

            @block.tensor
            def _(t, i=i, bi=bi):
                if i >= 2:
                    wait(t, s_v, v_lg[i - 2])
                for dc in range(DC):
                    t.matmul(psL[bi][:, :], hiT[:, dc * 128:(dc + 1) * 128], gwhi[:, dc, :],
                             start=(dc == 0), stop=False)
                    t.matmul(psL[bi][:, :], hiT[:, dc * 128:(dc + 1) * 128], gwlo[:, dc, :],
                             start=False, stop=False)
                    inst = t.matmul(psL[bi][:, :], loT[:, dc * 128:(dc + 1) * 128], gwhi[:, dc, :],
                                    start=False, stop=(dc == DC - 1))
                s_t.inc(inst)
                t_lg[i] = s_t.n

            @block.vector
            def _(v, i=i, bi=bi):
                wait(v, s_t, t_lg[i])
                if i >= 2:
                    wait(v, s_s4, 16 * i)
                s_v.inc(v.tensor_copy(lgsb[bi][:, :], psL[bi][:, :]))
                v_lg[i] = s_v.n

            @block.gpsimd
            def _(g, i=i, bi=bi):
                wait(g, s_v, v_lg[i])
                s_s4.inc(g.dma_start(out=o_lg1[i * 128:(i + 1) * 128, :], in_=lgsb[bi][:, :]), 16)

        @block.gpsimd
        def _(g):
            wait(g, s_v, s_v.n)
            for dc in range(DC):
                s_s5.inc(g.dma_start(out=o_xn1T[dc * 128:(dc + 1) * 128, :],
                                     in_=uTsb[:, dc * TPC:(dc + 1) * TPC]), 16)
            wait(g, s_s5)
            wait(g, s_s1)
            wait(g, s_s2)
            wait(g, s_s3)
            wait(g, s_s4)
    return nc


def run_launchE(x, s_all, y_all, comb_all, ipw, ipb, gw1):
    xf = np.ascontiguousarray(x.reshape(T, D))
    nc = _get("E", build_launchE)
    in_maps = []
    for c in range(NC):
        in_maps.append({
            "x_sh": xf[c * TPC:(c + 1) * TPC], "s_bf": s_all[c * TPC:(c + 1) * TPC],
            "y_tbl": y_all, "comb": comb_all[c],
            "ipw": ipw, "ipb": ipb.reshape(1, QO), "gw1": gw1,
        })
    res = run_bass_kernel_spmd(nc, in_maps, core_ids=list(range(NC)))
    return res.results




NKT1 = S // 128
NQG1 = S // 512


def build_launchF():
    nc = bass.Bass("TRN2", target_bir_lowering=False, debug=False)
    qmine = nc.dram_tensor("qmine", [4, HD, S], BF16, kind="ExternalInput").ap()
    kmine = nc.dram_tensor("kmine", [4, HD, S], BF16, kind="ExternalInput").ap()
    vmine = nc.dram_tensor("vmine", [4, S, HD], BF16, kind="ExternalInput").ap()
    o_ao = nc.dram_tensor("o_ao", [4, S, HD], BF16, kind="ExternalOutput").ap()

    es = contextlib.ExitStack()
    with es:
        sb = lambda name, shape, dt: es.enter_context(nc.sbuf_tensor(name, shape, dt))
        psa = lambda name, shape, dt: es.enter_context(nc.psum_tensor(name, shape, dt))
        io128 = sb("io128", [128, 128], F32)
        cid1 = sb("cid1", [128, 1], F32)
        identf = sb("identf", [128, 128], F32)
        tri01 = sb("tri01", [128, 128], BF16)
        nshift = sb("nshift", [128, 1], F32)
        KT = [sb(f"KT{j}", [64, S], BF16) for j in range(2)]
        QT = [sb(f"QT{j}", [64, S], BF16) for j in range(2)]
        vaug = [sb(f"vaug{j}", [128, NKT1, 65], BF16) for j in range(2)]
        PT = [sb(f"PT{j}", [128, 512], BF16) for j in range(2)]
        oT = sb("oT", [65, 512], F32)
        linv = sb("linv", [128, 1], F32)
        ao = sb("ao", [128, 4, HD], BF16)
        psS = [psa(f"psS{j}", [128, 512], F32) for j in range(2)]
        psO = psa("psO", [65, 512], F32)
        psT = psa("psTf", [128, 65], F32)

        s_kq = [Sem(nc, es, "s_kq0"), Sem(nc, es, "s_kq1")]
        s_vv = [Sem(nc, es, "s_vv0"), Sem(nc, es, "s_vv1")]
        s_v = Sem(nc, es, "s_v")
        s_a = Sem(nc, es, "s_a")
        s_t = Sem(nc, es, "s_t")
        s_g = Sem(nc, es, "s_g")
        s_st = Sem(nc, es, "s_st")

        block = es.enter_context(nc.Block())

        @block.gpsimd
        def _(g):
            g.iota(io128[:, :], pattern=[[1, 128]], base=0, channel_multiplier=0,
                   allow_small_or_imprecise_dtypes=True)
            s_g.inc(g.iota(cid1[:, :], pattern=[[0, 1]], base=0, channel_multiplier=1,
                           allow_small_or_imprecise_dtypes=True))

        @block.vector
        def _(v):
            s_v.inc(v.memset(nshift[:, :], -12.0))
            wait(v, s_g)
            s_v.inc(v.tensor_scalar(identf[:, :], io128[:, :], cid1[:, :], None, OP.is_equal))
            s_v.inc(v.tensor_scalar(tri01[:, :], io128[:, :], cid1[:, :], None, OP.is_ge))
        v_pre = s_v.n

        def emit_pair_loads(p):
            pb = p % 2

            @block.sync
            def _(sy, p=p, pb=pb):
                if p >= 2:
                    wait(sy, s_t, t_pair[p - 2])
                s_kq[pb].inc(sy.dma_start(out=KT[pb][:, :], in_=kmine[p, :, :]), 16)
                s_kq[pb].inc(sy.dma_start(out=QT[pb][:, :], in_=qmine[p, :, :]), 16)

            @block.scalar
            def _(a, p=p, pb=pb):
                if p >= 2:
                    wait(a, s_t, t_pair[p - 2])
                if p < 2:
                    wait(a, s_v, 1)
                s_vv[pb].inc(a.dma_start(
                    out=vaug[pb][:, :, 0:64],
                    in_=vmine[p, :, :].rearrange("(kt pp) d -> pp kt d", pp=128)), 16)

            @block.vector
            def _(v, p=p, pb=pb):
                if p >= 2:
                    wait(v, s_t, t_pair[p - 2])
                s_v.inc(v.memset(vaug[pb][:, :, 64:65], 1.0))
                vmemset_n[p] = s_v.n

        t_pair = [0] * 4
        a_exp = {}
        t_sc = {}
        t_pv = {}
        v_norm = [0] * 4
        vmemset_n = {}
        v_mask = [0]
        v_oT = [0]
        t_tr = [0]
        v_norm_last = [0]
        prev_exp_last = [0]
        prev_pv_last = [0]

        emit_pair_loads(0)
        emit_pair_loads(1)

        for p in range(4):
            pb = p % 2
            if p >= 2:
                pass

            vload_n = 16 * (p // 2 + 1)
            kqload_n = 32 * (p // 2 + 1)

            for qg in range(NQG1):
                ncommon = 4 * qg + 2
                for kt in range(ncommon):
                    si = kt % 2

                    @block.tensor
                    def _(t, p=p, pb=pb, qg=qg, kt=kt, si=si):
                        if p == 0 and qg == 0 and kt == 0:
                            wait(t, s_v, v_pre)
                        wait(t, s_kq[pb], kqload_n)
                        key = (p, qg, kt - 2)
                        if key in a_exp:
                            wait(t, s_a, a_exp[key])
                        elif kt < 2 and (p, qg) != (0, 0):
                            wait(t, s_a, prev_exp_last[0])
                        s_t.inc(t.matmul(psS[si][:, :], KT[pb][:, kt * 128:(kt + 1) * 128],
                                         QT[pb][:, qg * 512:(qg + 1) * 512],
                                         start=True, stop=True))
                        t_sc[(p, qg, kt)] = s_t.n

                    @block.scalar
                    def _(a, p=p, pb=pb, qg=qg, kt=kt, si=si):
                        wait(a, s_t, t_sc[(p, qg, kt)])
                        key = (p, qg, kt - 2)
                        if key in t_pv:
                            wait(a, s_t, t_pv[key])
                        elif kt < 2 and (p, qg) != (0, 0):
                            wait(a, s_t, prev_pv_last[0])
                        s_a.inc(a.activation(out=PT[si][:, :], in_=psS[si][:, :], func=AF.Exp,
                                             bias=nshift[:, 0:1], scale=0.125))
                        a_exp[(p, qg, kt)] = s_a.n

                    @block.tensor
                    def _(t, p=p, pb=pb, qg=qg, kt=kt, si=si):
                        wait(t, s_a, a_exp[(p, qg, kt)])
                        if kt == 0:
                            wait(t, s_vv[pb], vload_n)
                            wait(t, s_v, vmemset_n[p])
                            if (p, qg) != (0, 0):
                                wait(t, s_v, v_norm_last[0])
                        s_t.inc(t.matmul(psO[:, :], vaug[pb][:, kt, :], PT[si][:, :],
                                         start=(kt == 0), stop=False))
                        t_pv[(p, qg, kt)] = s_t.n

                all_extras = []
                for sq4 in range(4):
                    sq = 4 * qg + sq4
                    ex = [kt for kt in range(ncommon, sq + 3) if kt < NKT1]
                    for kt in ex:
                        all_extras.append((sq4, kt))
                for eidx, (sq4, kt) in enumerate(all_extras):
                    sq = 4 * qg + sq4
                    if True:
                        last_extra = (eidx == len(all_extras) - 1)
                        masked = (kt == sq + 2)
                        si = kt % 2

                        @block.tensor
                        def _(t, p=p, pb=pb, qg=qg, sq4=sq4, kt=kt, si=si, masked=masked):
                            key = (p, qg, kt - 2) if kt - 2 >= ncommon else None
                            wait(t, s_a, s_a.n)
                            s_t.inc(t.matmul(psS[si][:, 0:128], KT[pb][:, kt * 128:(kt + 1) * 128],
                                             QT[pb][:, sq * 128:(sq + 1) * 128],
                                             start=True, stop=True))
                            t_sc[(p, qg, "x", sq4, kt)] = s_t.n

                        @block.scalar
                        def _(a, p=p, qg=qg, sq4=sq4, kt=kt, si=si):
                            wait(a, s_t, t_sc[(p, qg, "x", sq4, kt)])
                            wait(a, s_t, s_t.n)
                            s_a.inc(a.activation(out=PT[si][:, 0:128], in_=psS[si][:, 0:128],
                                                 func=AF.Exp, bias=nshift[:, 0:1], scale=0.125))
                            a_exp[(p, qg, "x", sq4, kt)] = s_a.n

                        if masked:
                            @block.vector
                            def _(v, p=p, qg=qg, sq4=sq4, kt=kt, si=si):
                                wait(v, s_a, a_exp[(p, qg, "x", sq4, kt)])
                                s_v.inc(v.tensor_tensor(PT[si][:, 0:128], PT[si][:, 0:128],
                                                        tri01[:, :], OP.mult))
                                v_mask[0] = s_v.n

                        @block.tensor
                        def _(t, p=p, pb=pb, qg=qg, sq4=sq4, kt=kt, si=si, masked=masked,
                              last_extra=last_extra):
                            if masked:
                                wait(t, s_v, v_mask[0])
                            else:
                                wait(t, s_a, a_exp[(p, qg, "x", sq4, kt)])
                            s_t.inc(t.matmul(psO[:, sq4 * 128:(sq4 + 1) * 128],
                                             vaug[pb][:, kt, :], PT[si][:, 0:128],
                                             start=False, stop=last_extra))
                            t_pv[(p, qg, "x", sq4, kt)] = s_t.n

                @block.vector
                def _(v, p=p, qg=qg):
                    wait(v, s_t, s_t.n)
                    if qg >= 1 or p >= 1:
                        wait(v, s_st, 16 * (p * NQG1 + qg))
                    s_v.inc(v.tensor_copy(oT[:, :], psO[:, :]))
                    v_oT[0] = s_v.n

                for sq4 in range(4):
                    @block.tensor
                    def _(t, p=p, qg=qg, sq4=sq4):
                        wait(t, s_v, v_oT[0])
                        if sq4 >= 1:
                            wait(t, s_v, v_norm_last[0])
                        s_t.inc(t.transpose(psT[:, :], oT[0:65, sq4 * 128:(sq4 + 1) * 128],
                                            identf[0:65, 0:65]))
                        t_tr[0] = s_t.n

                    @block.vector
                    def _(v, p=p, qg=qg, sq4=sq4):
                        wait(v, s_t, t_tr[0])
                        s_v.inc(v.reciprocal(linv[:, :], psT[:, 64:65]))
                        wait(v, s_v)
                        s_v.inc(v.tensor_scalar(ao[:, sq4, :], psT[:, 0:64], linv[:, 0:1], None, OP.mult))
                        v_norm_last[0] = s_v.n

                @block.gpsimd
                def _(g, p=p, qg=qg):
                    wait(g, s_v, v_norm_last[0])
                    s_st.inc(g.dma_start(
                        out=o_ao[p, qg * 512:(qg + 1) * 512, :].rearrange("(s pp) d -> pp s d", pp=128),
                        in_=ao[:, :, :]), 16)

                prev_exp_last[0] = s_a.n
                prev_pv_last[0] = s_t.n

            t_pair[p] = s_t.n
            v_norm[p] = v_norm_last[0]
            if p + 2 < 4:
                emit_pair_loads(p + 2)

        @block.gpsimd
        def _(g):
            wait(g, s_st)
    return nc


def run_launchF(qkv_all):
    nc = _get("F", build_launchF)
    qkv4 = qkv_all.reshape(B, S, 3, H1, HD)
    in_maps = []
    for c in range(NC):
        pairs = [(pp // H1, pp % H1) for pp in range(4 * c, 4 * c + 4)]
        qm = np.stack([qkv4[b, :, 0, h, :].T for (b, h) in pairs])
        km = np.stack([qkv4[b, :, 1, h, :].T for (b, h) in pairs])
        vm = np.stack([qkv4[b, :, 2, h, :] for (b, h) in pairs])
        in_maps.append({"qmine": np.ascontiguousarray(qm),
                        "kmine": np.ascontiguousarray(km),
                        "vmine": np.ascontiguousarray(vm)})
    res = run_bass_kernel_spmd(nc, in_maps, core_ids=list(range(NC)))
    return res.results



def bcast_free(ap, n, axis=1):
    return bass.AP(ap.tensor, ap.offset, [list(ap.ap[0]), [0, n], list(ap.ap[1])])



NBIS = 44


def build_launchG():
    nc = bass.Bass("TRN2", target_bir_lowering=False, debug=False)
    lg_all = nc.dram_tensor("lg_all", [T, E1], F32, kind="ExternalInput").ap()
    gb_in = nc.dram_tensor("gb", [1, E1], F32, kind="ExternalInput").ap()
    metaf = nc.dram_tensor("metaf", [1, 8], F32, kind="ExternalInput").ap()
    metai = nc.dram_tensor("metai", [1, 8], I32, kind="ExternalInput").ap()
    o_list1 = nc.dram_tensor("o_list1", [2 * C1, 2], U32, kind="ExternalOutput").ap()
    o_comb1 = nc.dram_tensor("o_comb1", [TPC, 4], F32, kind="ExternalOutput").ap()

    es = contextlib.ExitStack()
    with es:
        sb = lambda name, shape, dt: es.enter_context(nc.sbuf_tensor(name, shape, dt))
        psa = lambda name, shape, dt: es.enter_context(nc.psum_tensor(name, shape, dt))
        io128 = sb("io128", [128, 128], F32)
        cid1 = sb("cid1", [128, 1], F32)
        identb = sb("identb", [128, 128], BF16)
        identf = sb("identf", [128, 128], F32)
        me0 = sb("me0", [128, 1], F32)
        me1 = sb("me1", [128, 1], F32)
        gbb = sb("gbb", [128, E1], F32)
        mcore = sb("mcore", [1, 8], I32)
        ones128 = sb("ones128", [128, 1], F32)
        onesrow = sb("onesrow", [1, 128], F32)
        lgt = [sb(f"lgt{j}", [128, 4, E1], F32) for j in range(2)]
        st_ = sb("stG", [128, 8], F32)
        gates = sb("gates", [128, E1], F32)
        scores = sb("scores", [128, E1], F32)
        top8 = sb("top8", [128, 8], F32)
        tidx = sb("tidx", [128, 8], U32)
        tif = sb("tif", [128, 8], F32)
        ohtok = sb("ohtokG", [128, NTT, 2 * E1], BF16)
        ti_sb = sb("ti_sbG", [128, NTT, 2], F32)
        w_sb = sb("w_sbG", [128, NTT, 2], F32)
        WMtok = sb("WMtok", [128, NTT, E1], F32)
        wtmp = sb("wtmp", [128, E1], F32)
        lo_t = sb("lo_t", [128, E1], F32)
        hi_t = sb("hi_t", [128, E1], F32)
        mid_t = sb("mid_t", [128, E1], F32)
        cmp_big = sb("cmp_big", [128, NTT, E1], BF16)
        cnt_e = sb("cnt_e", [128, E1], F32)
        cnt_row = sb("cnt_row", [1, E1], F32)
        gec = sb("gec", [128, E1], F32)
        WIN16 = sb("WIN16", [16, T], BF16)
        CNT1 = sb("CNT1", [16, T], F32)
        wint = sb("wint", [128, E1], BF16)
        winT = sb("winTG", [128, 16], BF16)
        poscT = sb("poscTG", [128, 16], F32)
        tmp32 = sb("tmp32", [128, 2 * E1], F32)
        wp2 = sb("wp2G", [128, 4], F32)
        m2 = sb("m2G", [128, 2], F32)
        mw2 = sb("mw2G", [128, 2], F32)
        idxf = sb("idxfG", [128, 2], F32)
        idxu = sb("idxuG", [128, 2], U32)
        vals = [sb(f"valsG{j}", [128, 2], U32) for j in range(2)]
        tcol = sb("tcolG", [128, 1], F32)
        hugeu = sb("hugeuG", [128, 80], U32)
        comb_sb = sb("comb_sbG", [128, NTT, 4], F32)
        psOH = psa("psOH", [16, 128], BF16)
        psW = [psa("psW0", [16, 128], F32), psa("psW1", [128, 16], BF16), psa("psW2", [128, 16], F32)]
        psc = psa("pscG", [1, E1], F32)
        pscb = psa("pscbG", [128, E1], F32)

        s_l = [Sem(nc, es, "s_l0"), Sem(nc, es, "s_l1")]
        s_m = Sem(nc, es, "s_m")
        s_v = Sem(nc, es, "s_v")
        s_a = Sem(nc, es, "s_a")
        s_t = Sem(nc, es, "s_t")
        s_g = Sem(nc, es, "s_g")
        s_sc = Sem(nc, es, "s_sc")
        s_st = Sem(nc, es, "s_st")
        brG = es.enter_context(nc.gpsimd.register("brG"))

        block = es.enter_context(nc.Block())

        @block.sync
        def _(sy):
            s_m.inc(sy.dma_start(out=me0[:, :], in_=metaf[0:1, 0:1].partition_broadcast(128).squeeze(1)), 16)
            s_m.inc(sy.dma_start(out=me1[:, :], in_=metaf[0:1, 1:2].partition_broadcast(128).squeeze(1)), 16)
            s_m.inc(sy.dma_start(out=gbb[:, :], in_=gb_in.partition_broadcast(128).squeeze(1)), 16)
            s_m.inc(sy.dma_start(out=mcore[:, :], in_=metai[:, :]), 16)

        @block.gpsimd
        def _(g):
            g.iota(io128[:, :], pattern=[[1, 128]], base=0, channel_multiplier=0,
                   allow_small_or_imprecise_dtypes=True)
            s_g.inc(g.iota(cid1[:, :], pattern=[[0, 1]], base=0, channel_multiplier=1,
                           allow_small_or_imprecise_dtypes=True))
            g.reg_mov(brG, 2 * C1 - 1)
            s_g.inc(g.iota(hugeu[:, :], pattern=[[0, 80]], base=1000000000, channel_multiplier=0))
            wait(g, s_g)
            s_sc.inc(g.dma_start(out=o_list1.rearrange("(p a) c -> p (a c)", p=128), in_=hugeu[:, :]), 16)

        @block.vector
        def _(v):
            s_v.inc(v.memset(ones128[:, :], 1.0))
            s_v.inc(v.memset(onesrow[:, :], 1.0))
            wait(v, s_g, 1)
            s_v.inc(v.tensor_scalar(identf[:, :], io128[:, :], cid1[:, :], None, OP.is_equal))
            wait(v, s_v)
            s_v.inc(v.tensor_copy(identb[:, :], identf[:, :]))
        v_const = s_v.n

        v_stat = [0] * NTT
        v_tile2 = [0] * NTT
        a_tile = [0] * NTT
        t_bis = [0]
        v_winoh = [0] * NTT

        for j in range(NTT):
            lb2 = (j // 4) % 2
            if j % 4 == 0:
                @block.sync
                def _(sy, j=j, lb2=lb2):
                    if j >= 8:
                        wait(sy, s_v, v_tile2[j - 8 + 3])
                    s_l[lb2].inc(sy.dma_start(
                        out=lgt[lb2][:, :, :],
                        in_=lg_all[j * 128:(j + 4) * 128, :].rearrange("(q p) e -> p q e", p=128)), 16)

            @block.vector
            def _(v, j=j, lb2=lb2):
                if j % 4 == 0:
                    wait(v, s_l[lb2], 16 * (j // 8 + 1))
                s_v.inc(v.reduce_max(st_[:, 0:1], lgt[lb2][:, j % 4, :], axis=AX.X))
                wait(v, s_v)
                s_v.inc(v.tensor_scalar(st_[:, 1:2], st_[:, 0:1], -1.0, None, OP.mult))
                v_stat[j] = s_v.n

            @block.scalar
            def _(a, j=j, lb2=lb2):
                wait(a, s_v, v_stat[j])
                if j >= 1:
                    wait(a, s_v, v_tile2[j - 1])
                s_a.inc(a.activation(out=gates[:, :], in_=lgt[lb2][:, j % 4, :], func=AF.Exp,
                                     bias=st_[:, 1:2], accum_out=st_[:, 2:3]))
                a_tile[j] = s_a.n

            @block.vector
            def _(v, j=j, lb2=lb2):
                wait(v, s_a, a_tile[j])
                if j == 0:
                    wait(v, s_m, 64)
                s_v.inc(v.reciprocal(st_[:, 3:4], st_[:, 2:3]))
                wait(v, s_v)
                s_v.inc(v.scalar_tensor_tensor(scores[:, :], gates[:, :], st_[:, 3:4], gbb[:, :],
                                               OP.mult, OP.add))
                wait(v, s_v)
                s_v.inc(v.max(top8[:, :], scores[:, :]))
                wait(v, s_v)
                s_v.inc(v.max_index(tidx[:, :], top8[:, :], scores[:, :]))
                wait(v, s_v)
                s_v.inc(v.tensor_copy(tif[:, 0:2], tidx[:, 0:2]))
                wait(v, s_v)
                s_v.inc(v.tensor_scalar(ohtok[:, j, 0:E1], io128[:, 0:E1], tif[:, 0:1], None, OP.is_equal))
                s_v.inc(v.tensor_scalar(ohtok[:, j, E1:2 * E1], io128[:, 0:E1], tif[:, 1:2], None, OP.is_equal))
                s_v.inc(v.tensor_copy(ti_sb[:, j, :], tif[:, 0:2]))
                wait(v, s_v)
                s_v.inc(v.tensor_tensor(tmp32.rearrange("p (k e) -> p k e", k=2),
                                        ohtok[:, j, :].rearrange("p (k e) -> p k e", k=2),
                                        bcast_free(lgt[lb2][:, j % 4, :], 2), OP.mult))
                wait(v, s_v)
                s_v.inc(v.tensor_reduce(w_sb[:, j, :], tmp32.rearrange("p (k e) -> p k e", k=2),
                                        AX.X, OP.add))
                wait(v, s_v)
                s_v.inc(v.tensor_scalar(wtmp[:, :], ohtok[:, j, 0:E1], w_sb[:, j, 0:1], None, OP.mult))
                wait(v, s_v)
                s_v.inc(v.scalar_tensor_tensor(wtmp[:, :], ohtok[:, j, E1:2 * E1], w_sb[:, j, 1:2],
                                               wtmp[:, :], OP.mult, OP.add))
                wait(v, s_v)
                s_v.inc(v.tensor_tensor(gates[:, :], ohtok[:, j, 0:E1], ohtok[:, j, E1:2 * E1], OP.add))
                wait(v, s_v)
                s_v.inc(v.tensor_scalar(gates[:, :], gates[:, :], 1e30, -1e30, OP.mult, OP.add))
                wait(v, s_v)
                s_v.inc(v.tensor_tensor(WMtok[:, j, :], wtmp[:, :], gates[:, :], OP.add))
                v_tile2[j] = s_v.n

        @block.vector
        def _(v):
            wait(v, s_v)
            s_v.inc(v.memset(lo_t[:, :], -64.0))
            s_v.inc(v.memset(hi_t[:, :], 64.0))
        v_bis = [s_v.n]

        for it in range(NBIS):
            @block.vector
            def _(v, it=it):
                wait(v, s_v, v_bis[0])
                if it > 0:
                    wait(v, s_t, t_bis[0])
                    s_v.inc(v.tensor_scalar(gec[:, :], pscb[:, :], float(C1 + 1), None, OP.is_ge))
                    wait(v, s_v)
                    s_v.inc(v.tensor_tensor(cnt_e[:, :], mid_t[:, :], lo_t[:, :], OP.subtract))
                    wait(v, s_v)
                    s_v.inc(v.tensor_tensor(cnt_e[:, :], cnt_e[:, :], gec[:, :], OP.mult))
                    wait(v, s_v)
                    s_v.inc(v.tensor_tensor(lo_t[:, :], lo_t[:, :], cnt_e[:, :], OP.add))
                    wait(v, s_v)
                    s_v.inc(v.tensor_tensor(cnt_e[:, :], hi_t[:, :], mid_t[:, :], OP.subtract))
                    wait(v, s_v)
                    s_v.inc(v.tensor_tensor(cnt_e[:, :], cnt_e[:, :], gec[:, :], OP.mult))
                    wait(v, s_v)
                    s_v.inc(v.tensor_tensor(hi_t[:, :], mid_t[:, :], cnt_e[:, :], OP.add))
                    wait(v, s_v)
                s_v.inc(v.tensor_tensor(mid_t[:, :], lo_t[:, :], hi_t[:, :], OP.add))
                wait(v, s_v)
                s_v.inc(v.tensor_scalar(mid_t[:, :], mid_t[:, :], 0.5, None, OP.mult))
                wait(v, s_v)
                s_v.inc(v.tensor_tensor(cmp_big[:, :, :], WMtok[:, :, :],
                                        bcast_free(mid_t[:, :], NTT), OP.is_gt))
                wait(v, s_v)
                s_v.inc(v.tensor_reduce(cnt_e[:, :], cmp_big.rearrange("p t e -> p e t"),
                                        AX.X, OP.add))
                v_bis[0] = s_v.n

            @block.tensor
            def _(t, it=it):
                wait(t, s_v, v_bis[0])
                s_t.inc(t.matmul(psc[:, :], ones128[:, 0:1], cnt_e[:, :], start=True, stop=True))
            t_cnt = s_t.n

            @block.vector
            def _(v, it=it):
                wait(v, s_t, t_cnt)
                s_v.inc(v.tensor_copy(cnt_row[:, :], psc[:, :]))
                v_bis[0] = s_v.n

            @block.tensor
            def _(t, it=it):
                wait(t, s_v, v_bis[0])
                s_t.inc(t.matmul(pscb[:, :], onesrow[0:1, :], cnt_row[:, :], start=True, stop=True))
                t_bis[0] = s_t.n

        @block.vector
        def _(v):
            wait(v, s_t, t_bis[0])
            s_v.inc(v.tensor_scalar(gec[:, :], pscb[:, :], float(C1 + 1), None, OP.is_ge))
            wait(v, s_v)
            s_v.inc(v.tensor_tensor(cnt_e[:, :], hi_t[:, :], mid_t[:, :], OP.subtract))
            wait(v, s_v)
            s_v.inc(v.tensor_tensor(cnt_e[:, :], cnt_e[:, :], gec[:, :], OP.mult))
            wait(v, s_v)
            s_v.inc(v.tensor_tensor(hi_t[:, :], mid_t[:, :], cnt_e[:, :], OP.add))
        v_theta = s_v.n

        v_win = [0] * NTT
        t_oh = [0] * NTT
        for j in range(NTT):
            @block.vector
            def _(v, j=j):
                if j == 0:
                    wait(v, s_v, v_theta)
                if j >= 1:
                    wait(v, s_v, v_winoh[j - 1])
                s_v.inc(v.tensor_tensor(wint[:, :], WMtok[:, j, :], hi_t[:, :], OP.is_gt))
                v_win[j] = s_v.n

            @block.tensor
            def _(t, j=j):
                wait(t, s_v, v_win[j])
                if j == 0:
                    wait(t, s_v, v_const)
                s_t.inc(t.transpose(psOH[:, :], wint[:, 0:E1], identb[:, :]))
                t_oh[j] = s_t.n

            @block.vector
            def _(v, j=j):
                wait(v, s_t, t_oh[j])
                s_v.inc(v.tensor_copy(WIN16[:, j * 128:(j + 1) * 128], psOH[0:16, :]))
                v_winoh[j] = s_v.n

        @block.vector
        def _(v):
            wait(v, s_v)
            s_v.inc(v.tensor_tensor_scan(CNT1[:, :], WIN16[:, :], WIN16[:, :], 0.0, OP.add, OP.bypass))
            wait(v, s_v)
            s_v.inc(v.tensor_scalar(CNT1[:, :], CNT1[:, :], -1.0, float(C1 - 1), OP.add, OP.min))
        v_scan = s_v.n

        t_ext = [0] * NTT
        v_ext = [0] * NTT
        for j in range(NTT):
            @block.tensor
            def _(t, j=j):
                if j == 0:
                    wait(t, s_v, v_scan)
                if j >= 1:
                    wait(t, s_v, v_ext[j - 1])
                s_t.inc(t.transpose(psW[1][:, :], WIN16[0:16, j * 128:(j + 1) * 128], identb[0:16, 0:16]))
                s_t.inc(t.transpose(psW[2][:, :], CNT1[0:16, j * 128:(j + 1) * 128], identf[0:16, 0:16]))
                t_ext[j] = s_t.n

            @block.vector
            def _(v, j=j):
                wait(v, s_t, t_ext[j])
                if j >= 1:
                    wait(v, s_sc, 16 * (1 + 2 * j))
                s_v.inc(v.tensor_copy(winT[:, :], psW[1][:, :]))
                s_v.inc(v.tensor_copy(poscT[:, :], psW[2][:, :]))
                wait(v, s_v)
                s_v.inc(v.tensor_tensor(tmp32[:, 0:E1], ohtok[:, j, 0:E1], winT[:, 0:E1], OP.mult))
                s_v.inc(v.tensor_tensor(tmp32[:, E1:2 * E1], ohtok[:, j, E1:2 * E1], winT[:, 0:E1], OP.mult))
                wait(v, s_v)
                s_v.inc(v.tensor_reduce(wp2[:, 0:2], tmp32.rearrange("p (k e) -> p k e", k=2), AX.X, OP.add))
                wait(v, s_v)
                s_v.inc(v.tensor_tensor(tmp32[:, 0:E1], ohtok[:, j, 0:E1], poscT[:, 0:E1], OP.mult))
                s_v.inc(v.tensor_tensor(tmp32[:, E1:2 * E1], ohtok[:, j, E1:2 * E1], poscT[:, 0:E1], OP.mult))
                wait(v, s_v)
                s_v.inc(v.tensor_reduce(wp2[:, 2:4], tmp32.rearrange("p (k e) -> p k e", k=2), AX.X, OP.add))
                wait(v, s_v)
                s_v.inc(v.tensor_tensor(comb_sb[:, j, 0:2], w_sb[:, j, :], wp2[:, 0:2], OP.mult))
                s_v.inc(v.tensor_scalar(idxf[:, :], ti_sb[:, j, :], float(C1), None, OP.mult))
                wait(v, s_v)
                s_v.inc(v.tensor_tensor(comb_sb[:, j, 2:4], idxf[:, :], wp2[:, 2:4], OP.add))
                s_v.inc(v.tensor_scalar(m2[:, :], ti_sb[:, j, :], me0[:, 0:1], None, OP.is_equal))
                wait(v, s_v)
                s_v.inc(v.tensor_tensor(mw2[:, :], m2[:, :], wp2[:, 0:2], OP.mult))
                wait(v, s_v)
                s_v.inc(v.tensor_scalar(idxf[:, :], wp2[:, 2:4], 100000.0, None, OP.subtract))
                wait(v, s_v)
                s_v.inc(v.tensor_tensor(idxf[:, :], idxf[:, :], mw2[:, :], OP.mult))
                wait(v, s_v)
                s_v.inc(v.tensor_scalar(m2[:, :], ti_sb[:, j, :], me1[:, 0:1], None, OP.is_equal))
                wait(v, s_v)
                s_v.inc(v.tensor_tensor(mw2[:, :], m2[:, :], wp2[:, 0:2], OP.mult))
                wait(v, s_v)
                s_v.inc(v.tensor_scalar(tmp32[:, 0:2], wp2[:, 2:4], float(100000 - C1), None, OP.subtract))
                wait(v, s_v)
                s_v.inc(v.tensor_tensor(tmp32[:, 0:2], tmp32[:, 0:2], mw2[:, :], OP.mult))
                wait(v, s_v)
                s_v.inc(v.tensor_tensor(idxf[:, :], idxf[:, :], tmp32[:, 0:2], OP.add))
                wait(v, s_v)
                s_v.inc(v.tensor_scalar(idxf[:, :], idxf[:, :], 100000.0, None, OP.add))
                wait(v, s_v)
                s_v.inc(v.tensor_copy(idxu[:, :], idxf[:, :]))
                s_v.inc(v.tensor_scalar(tcol[:, :], cid1[:, :], float(j * 128), None, OP.add))
                wait(v, s_v)
                s_v.inc(v.tensor_copy(vals[0][:, 0:1], tcol[:, :]))
                s_v.inc(v.tensor_copy(vals[1][:, 0:1], tcol[:, :]))
                s_v.inc(v.tensor_copy(vals[0][:, 1:2].bitcast(F32), comb_sb[:, j, 0:1]))
                s_v.inc(v.tensor_copy(vals[1][:, 1:2].bitcast(F32), comb_sb[:, j, 1:2]))
                v_ext[j] = s_v.n

            @block.gpsimd
            def _(g, j=j):
                wait(g, s_v, v_ext[j])
                if j == 0:
                    wait(g, s_sc, 16)
                s_sc.inc(g.indirect_dma_start(
                    out=o_list1, out_offset=IndirectOffsetOnAxis(ap=idxu[:, 0:1], axis=0),
                    in_=vals[0][:, :], in_offset=None,
                    bounds_check=brG, oob_is_err=False), 16)
                s_sc.inc(g.indirect_dma_start(
                    out=o_list1, out_offset=IndirectOffsetOnAxis(ap=idxu[:, 1:2], axis=0),
                    in_=vals[1][:, :], in_offset=None,
                    bounds_check=brG, oob_is_err=False), 16)

        @block.gpsimd
        def _(g):
            wait(g, s_v, s_v.n)
            with g.register("swrG") as swr:
                g.reg_load(swr, mcore[0:1, 1:2])
                for c in nc.Switch(engines=[g], index=[RuntimeValue(swr)], n=8):
                    s_st.inc(g.dma_start(
                        out=o_comb1.rearrange("(tl p) c -> p tl c", p=128),
                        in_=comb_sb[:, c * NT:(c + 1) * NT, :]), 16)
                    s_st.n -= 16
            s_st.n += 16
            wait(g, s_st)
            wait(g, s_sc)
    return nc


def run_launchG(lg1_all, gb):
    nc = _get("G", build_launchG)
    in_maps = []
    for c in range(NC):
        mf = np.zeros((1, 8), np.float32); mf[0, 0] = 2 * c; mf[0, 1] = 2 * c + 1
        mi = np.zeros((1, 8), np.int32); mi[0, 1] = c
        in_maps.append({"lg_all": lg1_all, "gb": gb.reshape(1, E1), "metaf": mf, "metai": mi})
    res = run_bass_kernel_spmd(nc, in_maps, core_ids=list(range(NC)))
    return res.results




H1F = 2048
HC1 = H1F // 128
NHT = C1 // 256


def build_launchH():
    nc = bass.Bass("TRN2", target_bir_lowering=False, debug=False)
    lst1 = nc.dram_tensor("lst1", [2 * C1, 2], U32, kind="ExternalInput").ap()
    xn1_tbl = nc.dram_tensor("xn1_tbl", [T, D], BF16, kind="ExternalInput").ap()
    ew1 = nc.dram_tensor("ew1", [2, D, H1F], F32, kind="ExternalInput").ap()
    ew3 = nc.dram_tensor("ew3", [2, D, H1F], F32, kind="ExternalInput").ap()
    ew2 = nc.dram_tensor("ew2", [2, H1F, D], F32, kind="ExternalInput").ap()
    o_y1 = nc.dram_tensor("o_y1", [2 * C1, D], BF16, kind="ExternalOutput").ap()

    es = contextlib.ExitStack()
    with es:
        sb = lambda name, shape, dt: es.enter_context(nc.sbuf_tensor(name, shape, dt))
        psa = lambda name, shape, dt: es.enter_context(nc.psum_tensor(name, shape, dt))
        io128 = sb("io128", [128, 128], F32)
        cid1 = sb("cid1", [128, 1], F32)
        identb = sb("identb", [128, 128], BF16)
        identf = sb("identf", [128, 128], F32)
        w1sb = sb("w1sb", [128, DC, H1F], BF16)
        w3sb = sb("w3sb", [128, DC, H1F], BF16)
        w2sb = sb("w2sb", [128, HC1, D], BF16)
        lidu = [sb(f"lidu{j}", [128, 2, 2], U32) for j in range(2)]
        xg = [sb(f"xg{j}", [128, D], BF16) for j in range(2)]
        xT = [sb(f"xT{j}", [128, DC, 256], BF16) for j in range(2)]
        t1f = sb("t1f", [128, 256], F32)
        hT = sb("hT", [128, HC1, 256], BF16)
        yT = sb("yT", [128, DC, 256], BF16)
        yout = sb("yout", [128, 2, D], BF16)
        psH = [psa(f"psH{j}", [128, 256], F32) for j in range(2)]
        psY = [psa(f"psY{j}", [128, 256], F32) for j in range(2)]
        psB = [psa(f"psB{j}", [128, 128], BF16) for j in range(2)]

        s_w = Sem(nc, es, "s_w")
        s_li = [Sem(nc, es, "s_li0"), Sem(nc, es, "s_li1")]
        s_ga = Sem(nc, es, "s_ga")
        s_v = Sem(nc, es, "s_v")
        s_a = Sem(nc, es, "s_a")
        s_t = Sem(nc, es, "s_t")
        s_g = Sem(nc, es, "s_g")
        s_st = Sem(nc, es, "s_st")
        brH = es.enter_context(nc.gpsimd.register("brH"))

        block = es.enter_context(nc.Block())

        @block.gpsimd
        def _(g):
            g.iota(io128[:, :], pattern=[[1, 128]], base=0, channel_multiplier=0,
                   allow_small_or_imprecise_dtypes=True)
            s_g.inc(g.iota(cid1[:, :], pattern=[[0, 1]], base=0, channel_multiplier=1,
                           allow_small_or_imprecise_dtypes=True))
            g.reg_mov(brH, T - 1)

        @block.vector
        def _(v):
            wait(v, s_g)
            s_v.inc(v.tensor_scalar(identf[:, :], io128[:, :], cid1[:, :], None, OP.is_equal))
            wait(v, s_v)
            s_v.inc(v.tensor_copy(identb[:, :], identf[:, :]))
            s_v.inc(v.memset(xg[0][:, :], 0.0))
            s_v.inc(v.memset(xg[1][:, :], 0.0))
        v_id = s_v.n

        t_x = {}
        t_h = {}
        t_y = {}
        v_tr = {}
        v_yev = {}
        a_g = {}
        a_y = {}
        tout_prev = [0]
        nst = [0]

        for el in range(2):
            @block.gpsimd
            def _(g, el=el):
                if el >= 1:
                    wait(g, s_t, t_y[(el - 1, NHT - 1)])
                s_w.inc(g.dma_start(out=w1sb[:, :, :],
                                    in_=ew1[el].rearrange("(dc p) h -> p dc h", p=128)), 16)
                s_w.inc(g.dma_start(out=w3sb[:, :, :],
                                    in_=ew3[el].rearrange("(dc p) h -> p dc h", p=128)), 16)
                s_w.inc(g.dma_start(out=w2sb[:, :, :],
                                    in_=ew2[el].rearrange("(hc p) d -> p hc d", p=128)), 16)

            for it2 in range(NHT):
                it = el * NHT + it2
                li = it % 2

                @block.scalar
                def _(a, el=el, it=it, li=li):
                    if it >= 2:
                        wait(a, s_ga, 32 * (it - 1))
                    s_li[li].inc(a.dma_start(
                        out=lidu[li][:, :, :],
                        in_=lst1[it * 256:(it + 1) * 256, :].rearrange("(s p) c -> p s c", p=128)), 16)

                @block.gpsimd
                def _(g, el=el, it=it, li=li):
                    wait(g, s_li[li], 16 * (it // 2 + 1))
                    if it >= 1:
                        wait(g, s_t, t_x[it - 1])
                    if it == 0:
                        wait(g, s_v, v_id)
                    for sub in range(2):
                        s_ga.inc(g.indirect_dma_start(
                            out=xg[sub][:, :], out_offset=None,
                            in_=xn1_tbl, in_offset=IndirectOffsetOnAxis(ap=lidu[li][:, sub, 0:1], axis=0),
                            bounds_check=brH, oob_is_err=False), 16)

                @block.tensor
                def _(t, el=el, it=it):
                    wait(t, s_ga, 32 * (it + 1))
                    if it >= 1:
                        wait(t, s_v, v_yev[it - 1])
                    vbase = s_v.n
                    for k in range(16):
                        sub, dc = divmod(k, DC)
                        if k >= 2:
                            wait(t, s_v, vbase + k - 1)
                        s_t.inc(t.transpose(psB[dc % 2][:, :], xg[sub][:, dc * 128:(dc + 1) * 128],
                                            identb[:, :]))
                    t_x[it] = s_t.n

                @block.vector
                def _(v, el=el, it=it):
                    base = t_x[it] - 16
                    for k in range(16):
                        sub, dc = divmod(k, DC)
                        wait(v, s_t, base + k + 1)
                        s_v.inc(v.tensor_copy(xT[it % 2][:, dc, sub * 128:(sub + 1) * 128],
                                              psB[dc % 2][:, :]))
                    v_tr[it] = s_v.n

                @block.tensor
                def _(t, el=el, it=it):
                    wait(t, s_v, v_tr[it])
                    if it == 0 or it2 == 0:
                        wait(t, s_w, 48 * (el + 1))
                    vhbase = s_v.n
                    abase = s_a.n
                    for hc in range(HC1):
                        if hc >= 1:
                            wait(t, s_a, abase + hc)
                            wait(t, s_v, vhbase + hc)
                        for dc in range(DC):
                            s_t.inc(t.matmul(psH[0][:, :], w1sb[:, dc, hc * 128:(hc + 1) * 128],
                                             xT[it % 2][:, dc, :], start=(dc == 0), stop=(dc == DC - 1))
                                    ) if dc == DC - 1 else t.matmul(
                                psH[0][:, :], w1sb[:, dc, hc * 128:(hc + 1) * 128],
                                xT[it % 2][:, dc, :], start=(dc == 0), stop=False)
                        for dc in range(DC):
                            s_t.inc(t.matmul(psH[1][:, :], w3sb[:, dc, hc * 128:(hc + 1) * 128],
                                             xT[it % 2][:, dc, :], start=(dc == 0), stop=(dc == DC - 1))
                                    ) if dc == DC - 1 else t.matmul(
                                psH[1][:, :], w3sb[:, dc, hc * 128:(hc + 1) * 128],
                                xT[it % 2][:, dc, :], start=(dc == 0), stop=False)
                    t_h[it] = s_t.n

                @block.scalar
                def _(a, el=el, it=it, it2=it2):
                    base = t_h[it] - 2 * HC1
                    multbase = v_tr[it]
                    if it >= 1:
                        wait(a, s_t, t_y[(el, it2 - 1) if it2 >= 1 else (el - 1, NHT - 1)])
                    for hc in range(HC1):
                        wait(a, s_t, base + 2 * hc + 1)
                        if hc >= 1:
                            wait(a, s_v, multbase + hc)
                        s_a.inc(a.activation(out=t1f[:, :], in_=psH[0][:, :], func=AF.Silu))
                    a_g[it] = s_a.n

                @block.vector
                def _(v, el=el, it=it):
                    abase = a_g[it] - HC1
                    tbase = t_h[it] - 2 * HC1
                    for hc in range(HC1):
                        wait(v, s_a, abase + hc + 1)
                        wait(v, s_t, tbase + 2 * hc + 2)
                        s_v.inc(v.tensor_tensor(hT[:, hc, :], t1f[:, :], psH[1][:, :], OP.mult))
                    v_h = s_v.n

                @block.tensor
                def _(t, el=el, it=it, it2=it2):
                    wait(t, s_v, s_v.n)
                    ybase = s_v.n
                    for dc2 in range(DC):
                        if dc2 >= 2:
                            wait(t, s_v, ybase + dc2 - 1)
                        for hc in range(HC1):
                            inst = t.matmul(psY[dc2 % 2][:, :], w2sb[:, hc, dc2 * 128:(dc2 + 1) * 128],
                                            hT[:, hc, :], start=(hc == 0), stop=(hc == HC1 - 1))
                        s_t.inc(inst)
                    t_y[(el, it2)] = s_t.n

                @block.vector
                def _(v, el=el, it=it, it2=it2):
                    base = t_y[(el, it2)] - DC
                    if it >= 1:
                        wait(v, s_t, tout_prev[0])
                    for dc2 in range(DC):
                        wait(v, s_t, base + dc2 + 1)
                        s_v.inc(v.tensor_copy(yT[:, dc2, :], psY[dc2 % 2][:, :]))
                    v_ye = s_v.n

                @block.tensor
                def _(t, el=el, it=it):
                    wait(t, s_v, s_v.n)
                    vbase2 = s_v.n
                    for k in range(16):
                        sub, dc2 = divmod(k, DC)
                        if k >= 2:
                            wait(t, s_v, vbase2 + k - 1)
                        s_t.inc(t.transpose(psB[dc2 % 2][:, :],
                                            yT[:, dc2, sub * 128:(sub + 1) * 128], identb[:, :]))
                    tout_prev[0] = s_t.n

                @block.vector
                def _(v, el=el, it=it):
                    base = tout_prev[0] - 16
                    if it >= 1:
                        wait(v, s_st, 16 * it)
                    for k in range(16):
                        sub, dc2 = divmod(k, DC)
                        wait(v, s_t, base + k + 1)
                        s_v.inc(v.tensor_copy(yout[:, sub, dc2 * 128:(dc2 + 1) * 128],
                                              psB[dc2 % 2][:, :]))
                    v_yev[it] = s_v.n

                @block.gpsimd
                def _(g, el=el, it=it):
                    wait(g, s_v, v_yev[it])
                    s_st.inc(g.dma_start(
                        out=o_y1[it * 256:(it + 1) * 256, :].rearrange("(s p) d -> p s d", p=128),
                        in_=yout[:, :, :]), 16)

        @block.gpsimd
        def _(g):
            wait(g, s_st)
    return nc


def run_launchH(lists1, xn1_all, EW1, EW3, EW2):
    nc = _get("H", build_launchH)
    in_maps = []
    for c in range(NC):
        in_maps.append({
            "lst1": lists1[c], "xn1_tbl": xn1_all,
            "ew1": EW1[2 * c:2 * c + 2], "ew3": EW3[2 * c:2 * c + 2], "ew2": EW2[2 * c:2 * c + 2],
        })
    res = run_bass_kernel_spmd(nc, in_maps, core_ids=list(range(NC)))
    return res.results





def build_launchI():
    nc = bass.Bass("TRN2", target_bir_lowering=False, debug=False)
    x1_sh = nc.dram_tensor("x1_sh", [TPC, D], F32, kind="ExternalInput").ap()
    xn1T_own = nc.dram_tensor("xn1T_own", [D, TPC], BF16, kind="ExternalInput").ap()
    aoT_in = nc.dram_tensor("aoT", [D, TPC], BF16, kind="ExternalInput").ap()
    opwT_in = nc.dram_tensor("opwT", [D, D], F32, kind="ExternalInput").ap()
    opb_in = nc.dram_tensor("opb", [1, D], F32, kind="ExternalInput").ap()
    sw1_in = nc.dram_tensor("sw1", [2, D, H1F], F32, kind="ExternalInput").ap()
    sw3_in = nc.dram_tensor("sw3", [2, D, H1F], F32, kind="ExternalInput").ap()
    sw2_in = nc.dram_tensor("sw2", [2, H1F, D], F32, kind="ExternalInput").ap()
    y1_tbl = nc.dram_tensor("y1_tbl", [E1 * C1, D], BF16, kind="ExternalInput").ap()
    comb1 = nc.dram_tensor("comb1", [TPC, 4], F32, kind="ExternalInput").ap()
    o_out = nc.dram_tensor("o_out", [TPC, D], F32, kind="ExternalOutput").ap()

    es = contextlib.ExitStack()
    with es:
        sb = lambda name, shape, dt: es.enter_context(nc.sbuf_tensor(name, shape, dt))
        psa = lambda name, shape, dt: es.enter_context(nc.psum_tensor(name, shape, dt))
        io128 = sb("io128", [128, 128], F32)
        cid1 = sb("cid1", [128, 1], F32)
        identb = sb("identb", [128, 128], BF16)
        identf = sb("identf", [128, 128], F32)
        aoT = sb("aoTs", [128, DC, TPC], BF16)
        shacc = sb("shacc", [128, DC, TPC], BF16)
        opbr = sb("opbr", [1, D], BF16)
        t1f = sb("t1fI", [128, 256], F32)
        hT = sb("hTI", [128, HC1, 256], BF16)
        psH = [psa(f"psHI{j}", [128, 256], F32) for j in range(2)]
        psY = [psa(f"psYI{j}", [128, 256], F32) for j in range(2)]
        psB = [psa(f"psBI{j}", [128, 128], BF16) for j in range(2)]
        psP = [psa(f"psPI{j}", [128, 512], F32) for j in range(2)]

        s_pre = Sem(nc, es, "s_pre")
        s_w = Sem(nc, es, "s_w")
        s_x = [Sem(nc, es, "s_x0"), Sem(nc, es, "s_x1")]
        s_ga = [Sem(nc, es, "s_ga0"), Sem(nc, es, "s_ga1")]
        s_v = Sem(nc, es, "s_v")
        s_a = Sem(nc, es, "s_a")
        s_t = Sem(nc, es, "s_t")
        s_g = Sem(nc, es, "s_g")
        s_st = Sem(nc, es, "s_st")
        brI = es.enter_context(nc.gpsimd.register("brI"))

        block = es.enter_context(nc.Block())

        @block.gpsimd
        def _(g):
            g.iota(io128[:, :], pattern=[[1, 128]], base=0, channel_multiplier=0,
                   allow_small_or_imprecise_dtypes=True)
            s_g.inc(g.iota(cid1[:, :], pattern=[[0, 1]], base=0, channel_multiplier=1,
                           allow_small_or_imprecise_dtypes=True))
            g.reg_mov(brI, E1 * C1 - 1)
            s_w.inc(g.dma_start(out=opbr[:, :], in_=opb_in[:, :]), 16)

        @block.sync
        def _(sy):
            s_pre.inc(sy.dma_start(out=aoT[:, :, :],
                                   in_=aoT_in.rearrange("(dc p) k -> p dc k", p=128)), 16)


        @block.vector
        def _(v):
            wait(v, s_g)
            s_v.inc(v.tensor_scalar(identf[:, :], io128[:, :], cid1[:, :], None, OP.is_equal))
            wait(v, s_v)
            s_v.inc(v.tensor_copy(identb[:, :], identf[:, :]))
        v_id = s_v.n

        es1 = contextlib.ExitStack()
        with es1:
            sb1 = lambda name, shape, dt: es1.enter_context(nc.sbuf_tensor(name, shape, dt))
            xnT = sb1("xnTI", [128, DC, TPC], BF16)
            w1sb = sb1("w1sbI", [128, DC, H1F], BF16)
            w3sb = sb1("w3sbI", [128, DC, H1F], BF16)
            w2sb = sb1("w2sbI", [128, HC1, D], BF16)

            @block.sync
            def _(sy):
                s_pre.inc(sy.dma_start(out=xnT[:, :, :],
                                       in_=xn1T_own.rearrange("(dc p) k -> p dc k", p=128)), 16)

            t_h = {}
            t_y = {}
            a_sil = {}
            v_hm = {}
            v_sh = {}
            prev_y = [None]

            for se in range(2):
                @block.gpsimd
                def _(g, se=se):
                    if se >= 1:
                        wait(g, s_t, t_y[(se - 1, DC - 1, TPC // 256 - 1)])
                    s_w.inc(g.dma_start(out=w1sb[:, :, :],
                                        in_=sw1_in[se].rearrange("(dc p) h -> p dc h", p=128)), 16)
                    s_w.inc(g.dma_start(out=w3sb[:, :, :],
                                        in_=sw3_in[se].rearrange("(dc p) h -> p dc h", p=128)), 16)
                    s_w.inc(g.dma_start(out=w2sb[:, :, :],
                                        in_=sw2_in[se].rearrange("(hc p) d -> p hc d", p=128)), 16)

                for ch in range(TPC // 256):
                    tok = ch * 256

                    @block.tensor
                    def _(t, se=se, ch=ch, tok=tok):
                        if se == 0 and ch == 0:
                            wait(t, s_pre, 32)
                        wait(t, s_w, 16 + 48 * (se + 1))
                        vhbase = s_v.n
                        abase = s_a.n
                        for hc in range(HC1):
                            if hc >= 1:
                                wait(t, s_a, abase + hc)
                                wait(t, s_v, vhbase + hc)
                            for dc in range(DC):
                                inst = t.matmul(psH[0][:, :], w1sb[:, dc, hc * 128:(hc + 1) * 128],
                                                xnT[:, dc, tok:tok + 256],
                                                start=(dc == 0), stop=(dc == DC - 1))
                            s_t.inc(inst)
                            for dc in range(DC):
                                inst = t.matmul(psH[1][:, :], w3sb[:, dc, hc * 128:(hc + 1) * 128],
                                                xnT[:, dc, tok:tok + 256],
                                                start=(dc == 0), stop=(dc == DC - 1))
                            s_t.inc(inst)
                        t_h[(se, ch)] = s_t.n

                    @block.scalar
                    def _(a, se=se, ch=ch):
                        base = t_h[(se, ch)] - 2 * HC1
                        multbase = s_v.n
                        if prev_y[0] is not None:
                            wait(a, s_t, t_y[prev_y[0]])
                        for hc in range(HC1):
                            wait(a, s_t, base + 2 * hc + 1)
                            if hc >= 1:
                                wait(a, s_v, multbase + hc)
                            s_a.inc(a.activation(out=t1f[:, :], in_=psH[0][:, :], func=AF.Silu))
                        a_sil[(se, ch)] = s_a.n

                    @block.vector
                    def _(v, se=se, ch=ch):
                        abase = a_sil[(se, ch)] - HC1
                        tbase = t_h[(se, ch)] - 2 * HC1
                        for hc in range(HC1):
                            wait(v, s_a, abase + hc + 1)
                            wait(v, s_t, tbase + 2 * hc + 2)
                            s_v.inc(v.tensor_tensor(hT[:, hc, :], t1f[:, :], psH[1][:, :], OP.mult))
                        v_hm[(se, ch)] = s_v.n

                    @block.tensor
                    def _(t, se=se, ch=ch, tok=tok):
                        wait(t, s_v, v_hm[(se, ch)])
                        ybase = s_v.n
                        for dc2 in range(DC):
                            if dc2 >= 2:
                                wait(t, s_v, ybase + dc2 - 1)
                            for hc in range(HC1):
                                inst = t.matmul(psY[dc2 % 2][:, :], w2sb[:, hc, dc2 * 128:(dc2 + 1) * 128],
                                                hT[:, hc, :], start=(hc == 0), stop=(hc == HC1 - 1))
                            s_t.inc(inst)
                            t_y[(se, dc2, ch)] = s_t.n
                        prev_y[0] = (se, DC - 1, ch)

                    @block.vector
                    def _(v, se=se, ch=ch, tok=tok):
                        for dc2 in range(DC):
                            wait(v, s_t, t_y[(se, dc2, ch)])
                            if se == 0:
                                s_v.inc(v.tensor_scalar(shacc[:, dc2, tok:tok + 256],
                                                        psY[dc2 % 2][:, :], 0.5, None, OP.mult))
                            else:
                                s_v.inc(v.scalar_tensor_tensor(shacc[:, dc2, tok:tok + 256],
                                                               psY[dc2 % 2][:, :], 0.5,
                                                               shacc[:, dc2, tok:tok + 256],
                                                               OP.mult, OP.add))
                        v_sh[(se, ch)] = s_v.n
            v_shared_done = s_v.n
            t_shared_done = s_t.n

        es2 = contextlib.ExitStack()
        with es2:
            sb2 = lambda name, shape, dt: es2.enter_context(nc.sbuf_tensor(name, shape, dt))
            opwT = sb2("opwTs", [128, DC, D], BF16)
            combt = [sb2(f"combtI{j}", [128, 4], F32) for j in range(2)]
            giu = sb2("giuI", [128, 2], U32)
            yg = [sb2(f"ygI{j}", [128, D], BF16) for j in range(2)]
            x1t = [sb2(f"x1tI{j}", [128, D], F32) for j in range(2)]
            g1 = sb2("g1I", [128, D], F32)
            outt = [sb2(f"outtI{j}", [128, D], F32) for j in range(2)]
            ones1 = sb2("ones1I", [1, 128], BF16)

            @block.gpsimd
            def _(g):
                wait(g, s_t, t_shared_done)
                s_w.inc(g.dma_start(out=opwT[:, :, :],
                                    in_=opwT_in.rearrange("(dc p) d -> p dc d", p=128)), 16)

            @block.vector
            def _(v):
                wait(v, s_t, t_shared_done)
                s_v.inc(v.memset(ones1[:, :], 1.0))
                s_v.inc(v.memset(yg[0][:, :], 0.0))
                s_v.inc(v.memset(yg[1][:, :], 0.0))
            v_ph2 = s_v.n

            v_comb = [0] * NT
            v_gp = [0] * NT
            v_out = [0] * NT
            a_sil2 = [0] * NT
            t_sh = [0] * NT
            t_op = [0] * NT
            ga_n = [0] * NT

            for i in range(NT):
                bi = i % 2

                @block.sync
                def _(sy, i=i, bi=bi):
                    if i < 2:
                        wait(sy, s_t, t_shared_done)
                    if i >= 2:
                        wait(sy, s_v, v_out[i - 2])
                    s_x[bi].inc(sy.dma_start(out=combt[bi][:, :], in_=comb1[i * 128:(i + 1) * 128, :]), 16)
                    s_x[bi].inc(sy.dma_start(out=x1t[bi][:, :], in_=x1_sh[i * 128:(i + 1) * 128, :]), 16)

                @block.vector
                def _(v, i=i, bi=bi):
                    wait(v, s_x[bi], 32 * (i // 2 + 1))
                    if i >= 1:
                        wait(v, s_ga[(i - 1) % 2], ga_n[i - 1])
                    if i == 0:
                        wait(v, s_v, v_ph2)
                    s_v.inc(v.tensor_copy(giu[:, :], combt[bi][:, 2:4]))
                    v_comb[i] = s_v.n

                @block.gpsimd
                def _(g, i=i, bi=bi):
                    wait(g, s_v, v_comb[i])
                    if i >= 1:
                        wait(g, s_v, v_gp[i - 1])
                    for sub in range(2):
                        s_ga[bi].inc(g.indirect_dma_start(
                            out=yg[sub][:, :], out_offset=None,
                            in_=y1_tbl, in_offset=IndirectOffsetOnAxis(ap=giu[:, sub:sub + 1], axis=0),
                            bounds_check=brI, oob_is_err=False), 16)
                    ga_n[i] = s_ga[bi].n

                @block.tensor
                def _(t, i=i):
                    if i == 0:
                        wait(t, s_v, v_shared_done)
                    if i >= 1:
                        wait(t, s_v, v_gp[i - 1])
                    vsbase = s_v.n
                    for dc in range(DC):
                        if dc >= 2:
                            wait(t, s_v, vsbase + 2 + dc - 1)
                        s_t.inc(t.transpose(psB[dc % 2][:, :],
                                            shacc[:, dc, i * 128:(i + 1) * 128], identb[:, :]))
                    t_sh[i] = s_t.n

                @block.vector
                def _(v, i=i, bi=bi):
                    wait(v, s_ga[bi], ga_n[i])
                    s_v.inc(v.tensor_scalar(g1[:, :], yg[0][:, :], combt[bi][:, 0:1], None, OP.mult))
                    wait(v, s_v)
                    s_v.inc(v.scalar_tensor_tensor(g1[:, :], yg[1][:, :], combt[bi][:, 1:2],
                                                   g1[:, :], OP.mult, OP.add))
                    v_gp[i] = s_v.n
                    wait(v, s_v)
                    base = t_sh[i] - DC
                    for dc in range(DC):
                        wait(v, s_t, base + dc + 1)
                        s_v.inc(v.tensor_tensor(g1[:, dc * 128:(dc + 1) * 128],
                                                g1[:, dc * 128:(dc + 1) * 128],
                                                psB[dc % 2][:, :], OP.add))
                    v_gp[i] = s_v.n

                @block.scalar
                def _(a, i=i):
                    wait(a, s_v, v_gp[i])
                    s_a.inc(a.activation(out=g1[:, :], in_=g1[:, :], func=AF.Silu))
                    a_sil2[i] = s_a.n

                @block.tensor
                def _(t, i=i, bi=bi):
                    if i == 0:
                        wait(t, s_w, 128)
                        wait(t, s_v, v_ph2)
                    if i >= 2:
                        wait(t, s_v, v_out[i - 2])
                    for oq in range(2):
                        t.matmul(psP[oq][:, :], ones1[:, :], opbr[:, oq * 512:(oq + 1) * 512],
                                 start=True, stop=False)
                        for dc in range(DC):
                            inst = t.matmul(psP[oq][:, :], aoT[:, dc, i * 128:(i + 1) * 128],
                                            opwT[:, dc, oq * 512:(oq + 1) * 512],
                                            start=False, stop=(dc == DC - 1))
                        s_t.inc(inst)
                    t_op[i] = s_t.n

                @block.vector
                def _(v, i=i, bi=bi):
                    wait(v, s_a, a_sil2[i])
                    wait(v, s_t, t_op[i])
                    if i >= 2:
                        wait(v, s_st, 16 * i)
                    for oq in range(2):
                        s_v.inc(v.scalar_tensor_tensor(outt[bi][:, oq * 512:(oq + 1) * 512],
                                                       g1[:, oq * 512:(oq + 1) * 512], 1.0,
                                                       psP[oq][:, :], OP.mult, OP.mult))
                    wait(v, s_v)
                    for oq in range(2):
                        s_v.inc(v.tensor_tensor(outt[bi][:, oq * 512:(oq + 1) * 512],
                                                outt[bi][:, oq * 512:(oq + 1) * 512],
                                                x1t[bi][:, oq * 512:(oq + 1) * 512], OP.add))
                    v_out[i] = s_v.n

                @block.gpsimd
                def _(g, i=i, bi=bi):
                    wait(g, s_v, v_out[i])
                    s_st.inc(g.dma_start(out=o_out[i * 128:(i + 1) * 128, :], in_=outt[bi][:, :]), 16)

            @block.gpsimd
            def _(g):
                wait(g, s_st)
    return nc


def run_launchI(x1_all, xn1T_all, ao_all, opw, opb, SW1, SW3, SW2, y1_all, comb1_all):
    nc = _get("I", build_launchI)
    opwT = np.ascontiguousarray(opw.T)
    in_maps = []
    for c in range(NC):
        in_maps.append({
            "x1_sh": x1_all[c * TPC:(c + 1) * TPC],
            "xn1T_own": xn1T_all[c],
            "aoT": ao_all[c],
            "opwT": opwT, "opb": opb.reshape(1, D),
            "sw1": SW1, "sw3": SW3, "sw2": SW2,
            "y1_tbl": y1_all, "comb1": comb1_all[c],
        })
    res = run_bass_kernel_spmd(nc, in_maps, core_ids=list(range(NC)))
    return res.results


def kernel(**inputs):
    x = np.ascontiguousarray(inputs["L0_attn_x"], np.float32)
    Wg = np.ascontiguousarray(inputs["L0_ffn_Wg"], np.float32)
    W1 = np.ascontiguousarray(inputs["L0_ffn_W1"], np.float32)
    b1 = np.ascontiguousarray(inputs["L0_ffn_b1"], np.float32)
    W2 = np.ascontiguousarray(inputs["L0_ffn_W2"], np.float32)
    b2 = np.ascontiguousarray(inputs["L0_ffn_b2"], np.float32)
    ipw = np.ascontiguousarray(inputs["L1_attn_in_proj_w"], np.float32)
    ipb = np.ascontiguousarray(inputs["L1_attn_in_proj_b"], np.float32)
    opw = np.ascontiguousarray(inputs["L1_attn_out_proj_w"], np.float32)
    opb = np.ascontiguousarray(inputs["L1_attn_out_proj_b"], np.float32)
    gw1 = np.ascontiguousarray(inputs["L1_ffn_gate_w"], np.float32)
    gb = np.ascontiguousarray(inputs["L1_ffn_gate_b"], np.float32)
    EW1 = np.ascontiguousarray(inputs["L1_ffn_ew1"], np.float32)
    EW3 = np.ascontiguousarray(inputs["L1_ffn_ew3"], np.float32)
    EW2 = np.ascontiguousarray(inputs["L1_ffn_ew2"], np.float32)
    SW1 = np.ascontiguousarray(inputs["L1_ffn_sw1"], np.float32)
    SW3 = np.ascontiguousarray(inputs["L1_ffn_sw3"], np.float32)
    SW2 = np.ascontiguousarray(inputs["L1_ffn_sw2"], np.float32)

    A = run_launchA(x, Wg)
    xn_all = np.concatenate([r["o_xnbf"] for r in A], 0)
    uT_all = np.stack([r["o_uT"] for r in A])
    lg_all = np.ascontiguousarray(np.concatenate([r["o_lg"] for r in A], 0))

    Bo = run_launchB(uT_all, xn_all)
    s_all = np.concatenate([r["o_s"] for r in Bo], 0)

    Co = run_launchC(lg_all)
    lists = [r["o_list"] for r in Co]
    comb_all = [r["o_comb"] for r in Co]

    Do = run_launchD(lists, xn_all, W1, b1, W2, b2)
    y_all = np.concatenate([r["o_y"] for r in Do], 0)

    Eo = run_launchE(x, s_all, y_all, comb_all, ipw, ipb, gw1)
    x1_all = np.concatenate([r["o_x1"] for r in Eo], 0)
    xn1_all = np.concatenate([r["o_xn1"] for r in Eo], 0)
    xn1T_all = np.stack([r["o_xn1T"] for r in Eo])
    qkv_all = np.concatenate([r["o_qkv"] for r in Eo], 0)
    lg1_all = np.ascontiguousarray(np.concatenate([r["o_lg1"] for r in Eo], 0))

    Fo = run_launchF(qkv_all)
    ao_full = np.zeros((B, H1, S, HD), dtype=BF)
    for c in range(NC):
        aoc = Fo[c]["o_ao"].reshape(4, S, HD)
        for u in range(4):
            pp = 4 * c + u
            ao_full[pp // H1, pp % H1] = aoc[u]
    ao_tok = ao_full.transpose(0, 2, 1, 3).reshape(T, D)
    ao_all = np.stack([np.ascontiguousarray(ao_tok[c * TPC:(c + 1) * TPC].T) for c in range(NC)])

    Go = run_launchG(lg1_all, gb)
    lists1 = [r["o_list1"] for r in Go]
    comb1_all = [r["o_comb1"] for r in Go]

    Ho = run_launchH(lists1, xn1_all, EW1, EW3, EW2)
    y1_all = np.concatenate([r["o_y1"] for r in Ho], 0)

    Io = run_launchI(x1_all, xn1T_all, ao_all, opw, opb, SW1, SW3, SW2, y1_all, comb1_all)
    out = np.concatenate([r["o_out"] for r in Io], 0)
    return np.ascontiguousarray(out.reshape(B, S, D))
